# version 33
# baseline (speedup 1.0000x reference)
import numpy as np
import concourse.bass as bass
import concourse.mybir as mybir
from concourse.bass_utils import run_bass_kernel_spmd

F32 = mybir.dt.float32
BF16 = mybir.dt.bfloat16
AF = mybir.ActivationFunctionType
ALU = mybir.AluOpType
AX = mybir.AxisListType.X

D = 1024
DC = 8
NH = 16
DH = 64
WIN = 128
TC = 4096
NBLK = TC // WIN
HF = 4096
HCN = 32
TB = 256
NST = TC // TB

ENGS = ("te", "ve", "se", "sy", "gp")
INTERLEAVE = True
INTERLEAVE_K = 16
DSEMS = ("sx0", "sx1", "swa", "swg", "slt", "sst0", "sst1",
         "swb1", "swb2", "sbi0", "sbi1", "sso0", "sso1") + tuple(
    f"swA{i}" for i in range(8)) + tuple(f"swB{i}" for i in range(8))


class Res:
    __slots__ = ("w", "r")

    def __init__(self):
        self.w = None
        self.r = []


def _rect(x):
    if not isinstance(x, bass.AP):
        return None
    t = x.tensor
    ncol = int(t.shape[-1])
    off = int(x.offset)
    p_ext = 0
    c_ext = 0
    for pr in x.ap:
        stride, n = int(pr[0]), int(pr[1])
        if n <= 1:
            continue
        if stride >= ncol and stride % ncol == 0:
            p_ext += (stride // ncol) * (n - 1)
        else:
            c_ext += stride * (n - 1)
    p0 = off // ncol
    c0 = off % ncol
    return (t.name, p0, p0 + p_ext + 1, c0, c0 + c_ext + 1)


_TRACKED = ("ve", "se")


class Prog:
    def __init__(self, mode, ck, filt=None, engs=None, esem=None, dsem=None):
        self.mode = mode
        self.ck = ck
        self.filt = filt
        self.engs = engs or {}
        self.esem = esem or {}
        self.dsem = dsem or {}
        self.cnt = {e: 0 for e in ENGS}
        self.dcnt = {s: 0 for s in DSEMS}
        self.shadow = {e: {} for e in _TRACKED}
        self.self_hw = {e: 0 for e in _TRACKED}

    def active(self, eng):
        return self.mode == "emit" and self.filt == eng

    def wait_ck(self, eng, name):
        if name is None:
            return
        kind, key, val = self.ck[name]
        if kind == "eng":
            if key == eng or val == 0:
                return
            if self.active(eng):
                self.engs[eng].wait_ge(self.esem[key], val)
        else:
            if val == 0:
                return
            if self.active(eng):
                self.engs[eng].wait_ge(self.dsem[key], val)

    def wait_dma(self, eng, dsname, val):
        if val and self.active(eng):
            self.engs[eng].wait_ge(self.dsem[dsname], val)

    def op(self, eng, meth, *a, inc=False, **k):
        if eng in _TRACKED:
            wr, rd = [], []
            for i, x in enumerate(a):
                r = _rect(x)
                if r is not None:
                    (wr if i == 0 else rd).append(r)
            for key in ("bias", "scale"):
                r = _rect(k.get(key))
                if r is not None:
                    rd.append(r)
            r = _rect(k.get("accum_out"))
            if r is not None:
                wr.append(r)
            need = 0
            sh = self.shadow[eng]
            for (tn, p0, p1, c0, c1) in rd:
                for (isw, q0, q1, d0, d1), val in sh.get(tn, {}).items():
                    if (isw and val > need and p0 < q1 and q0 < p1
                            and c0 < d1 and d0 < c1):
                        need = val
            for (tn, p0, p1, c0, c1) in wr:
                for (isw, q0, q1, d0, d1), val in sh.get(tn, {}).items():
                    if (val > need and p0 < q1 and q0 < p1
                            and c0 < d1 and d0 < c1):
                        need = val
            if need > self.self_hw[eng]:
                self.self_hw[eng] = need
                if self.active(eng):
                    self.engs[eng].wait_ge(self.esem[eng], need)
            self.cnt[eng] += 1
            v = self.cnt[eng]
            for (tn, p0, p1, c0, c1) in wr:
                d = sh.setdefault(tn, {})
                d[(True, p0, p1, c0, c1)] = v
                if len(d) > 96:
                    hw = self.self_hw[eng]
                    for kk in [kk for kk, vv in d.items() if vv <= hw]:
                        del d[kk]
            for (tn, p0, p1, c0, c1) in rd:
                d = sh.setdefault(tn, {})
                d[(False, p0, p1, c0, c1)] = v
                if len(d) > 96:
                    hw = self.self_hw[eng]
                    for kk in [kk for kk, vv in d.items() if vv <= hw]:
                        del d[kk]
            ins = None
            if self.active(eng):
                ins = getattr(self.engs[eng], meth)(*a, **k)
                ins.then_inc(self.esem[eng], 1)
            return ins
        ins = None
        if self.active(eng):
            ins = getattr(self.engs[eng], meth)(*a, **k)
        if inc:
            self.cnt[eng] += 1
            if ins is not None:
                ins.then_inc(self.esem[eng], 1)
        return ins

    def dma(self, eng, dsname, out, in_, **kw):
        self.dcnt[dsname] += 16
        if self.active(eng):
            self.engs[eng].dma_start(out=out, in_=in_, **kw).then_inc(
                self.dsem[dsname], 16)

    def ckpt(self, eng, name):
        if self.mode == "count":
            assert name not in self.ck, name
            self.ck[name] = ("eng", eng, self.cnt[eng])

    def dma_ckpt(self, dsname, name):
        if self.mode == "count":
            assert name not in self.ck, name
            self.ck[name] = ("dma", dsname, self.dcnt[dsname])


def acquire(P, eng, reads=(), writes=()):
    names = []
    for x in reads:
        names.append(x.w)
    for x in writes:
        names.append(x.w)
        names.extend(x.r)
    for n in dict.fromkeys(names):
        P.wait_ck(eng, n)


def commit(P, eng, name, reads=(), writes=()):
    P.ckpt(eng, name)
    for x in reads:
        x.r.append(name)
    for x in writes:
        x.w = name
        x.r = []


def commit_dma(P, dsname, name, reads=(), writes=()):
    P.dma_ckpt(dsname, name)
    for x in reads:
        x.r.append(name)
    for x in writes:
        x.w = name
        x.r = []


def build_nc(debug=False):
    nc = bass.Bass("TRN2", target_bir_lowering=False)
    scr_kind = {"kind": "ExternalOutput"} if debug else {}

    xd = nc.dram_tensor("x", [TC, D], F32, kind="ExternalInput")
    xhd = nc.dram_tensor("xh", [WIN, D], F32, kind="ExternalInput")
    xled = nc.dram_tensor("xle", [8, D], F32, kind="ExternalInput")
    xlod = nc.dram_tensor("xlo", [8, D], F32, kind="ExternalInput")
    wqkvd = nc.dram_tensor("wqkv", [D, 3 * D], F32, kind="ExternalInput")
    wod = nc.dram_tensor("wo", [D, D], F32, kind="ExternalInput")
    w1td = nc.dram_tensor("w1t", [D, HF], F32, kind="ExternalInput")
    w2td = nc.dram_tensor("w2t", [HF, D], F32, kind="ExternalInput")
    gmd = nc.dram_tensor("gm", [128, 8], F32, kind="ExternalInput")
    b1md = nc.dram_tensor("b1m", [128, 32], F32, kind="ExternalInput")
    b2md = nc.dram_tensor("b2m", [128, 8], F32, kind="ExternalInput")
    scd = nc.dram_tensor("scv", [1, 1], F32, kind="ExternalInput")
    m0d = nc.dram_tensor("mask0", [128, 512], F32, kind="ExternalInput")
    mrd = nc.dram_tensor("maskr", [128, 512], F32, kind="ExternalInput")
    identd = nc.dram_tensor("ident", [128, 128], F32, kind="ExternalInput")
    scr_r = nc.dram_tensor("scr_r", [D, TC], F32, **scr_kind)
    dbgst = nc.dram_tensor("dbgst", [128, 80], F32, **scr_kind) if debug else None
    scr_xn = nc.dram_tensor("scr_xn", [D, TC], BF16, **scr_kind)
    outd = nc.dram_tensor("out", [D, TC], F32, kind="ExternalOutput")

    ck = {}

    import contextlib
    with contextlib.ExitStack() as stack:
        def sb(name, shape, dtype):
            return stack.enter_context(nc.sbuf_tensor(name, shape, dtype))

        def sem(name):
            return stack.enter_context(nc.semaphore(name))

        gm_sb = sb("gm_sb", [128, 8], F32)
        b1_sb = sb("b1_sb", [128, 32], F32)
        b2_sb = sb("b2_sb", [128, 8], F32)
        sc8 = sb("sc8", [8, 1], F32)
        idf = sb("idf", [128, 128], F32)
        idb = sb("idb", [128, 128], BF16)
        m0b = sb("m0b", [128, 512], BF16)
        mrb = sb("mrb", [128, 512], BF16)
        ev1zb = sb("ev1zb", [4, HF], BF16)
        ev2zb = sb("ev2zb", [4, HF], BF16)
        oddtb = sb("oddtb", [128, 64], BF16)
        stats = sb("stats", [128, 80], F32)
        negmx = sb("negmx", [128, 4], F32)
        ssum = sb("ssum", [128, 4], F32)
        srec = sb("srec", [128, 4], F32)
        s_te = sem("s_te")
        s_ve = sem("s_ve")
        s_se = sem("s_se")
        dsh = {n: sem("d_" + n) for n in DSEMS}

        esem = {"te": s_te, "ve": s_ve, "se": s_se}
        dsem = dsh

        with contextlib.ExitStack() as stA:
            def sa(name, shape, dtype):
                return stA.enter_context(nc.sbuf_tensor(name, shape, dtype))

            def pa(name, shape, dtype):
                return stA.enter_context(nc.psum_tensor(name, shape, dtype))

            wqkv_b = sa("wqkv_b", [128, DC * 3 * D], BF16)
            wo_b = sa("wo_b", [128, DC * D], BF16)

            xbuf0 = sa("xbuf0", [128, D], F32)
            xbuf1 = sa("xbuf1", [128, D], F32)
            xc0 = sa("xc0", [128, D], F32)
            xc1 = sa("xc1", [128, D], F32)
            xn2b0 = sa("xn2b0", [128, D], BF16)
            xn2b1 = sa("xn2b1", [128, D], BF16)
            xnb0 = sa("xnb0", [128, D], BF16)
            xnb1 = sa("xnb1", [128, D], BF16)
            r320 = sa("r320", [128, D], F32)
            r321 = sa("r321", [128, D], F32)
            xn2tP = sa("xn2tP", [128, 2048], BF16)
            xntP0 = sa("xntP0", [128, 2048], BF16)
            xntP1 = sa("xntP1", [128, 2048], BF16)
            xntP = (xntP0, xntP1)
            rtP0 = sa("rtP0", [128, 2048], F32)
            rtP1 = sa("rtP1", [128, 2048], F32)
            rtP = (rtP0, rtP1)
            qtP0 = sa("qtP0", [128, 2048], BF16)
            qtP1 = sa("qtP1", [128, 2048], BF16)
            kvKP0 = sa("kvKP0", [128, 3072], BF16)
            kvKP1 = sa("kvKP1", [128, 3072], BF16)
            kvVP0 = sa("kvVP0", [128, 2048], BF16)
            kvVP1 = sa("kvVP1", [128, 2048], BF16)
            ebP0 = sa("ebP0", [128, 512], BF16)
            ebP1 = sa("ebP1", [128, 512], BF16)
            ptbP0 = sa("ptbP0", [128, 512], BF16)
            ptbP1 = sa("ptbP1", [128, 512], BF16)
            dgb0 = sa("dgb0", [128, 128], BF16)
            dgb1 = sa("dgb1", [128, 128], BF16)
            dgb2 = sa("dgb2", [128, 128], BF16)
            dgb3 = sa("dgb3", [128, 128], BF16)
            otP0 = sa("otP0", [128, 2048], BF16)
            otP1 = sa("otP1", [128, 2048], BF16)
            routP0 = sa("routP0", [128, 2048], F32)
            routP1 = sa("routP1", [128, 2048], F32)
            routP = (routP0, routP1)
            PS0 = pa("PS0", [128, 512], F32)
            PS1 = pa("PS1", [128, 512], F32)
            PP0 = pa("PP0", [128, 512], F32)
            PP1 = pa("PP1", [128, 512], F32)
            PO0 = pa("PO0", [128, 512], F32)
            PO1 = pa("PO1", [128, 512], F32)
            PTb0 = pa("PTb0", [128, 1024], BF16)
            PV0 = pa("PV0", [128, 512], F32)
            PV1 = PV0

            xbuf = (xbuf0, xbuf1)
            xcb = (xc0, xc1)
            xn2b = (xn2b0, xn2b1)
            xnb = (xnb0, xnb1)
            r32 = (r320, r321)
            qtP = (qtP0, qtP1)
            kvKP = (kvKP0, kvKP1)
            kvVP = (kvVP0, kvVP1)
            otP = (otP0, otP1)
            ebP = (ebP0, ebP1)
            ptbP = (ptbP0, ptbP1)
            dgb = (dgb0, dgb1, dgb2, dgb3)

            PS = (PS0, PS1)
            PP = (PP0, PP1)
            PO = (PO0, PO1)
            PV = (PV0, PV1)

            def scol(slot, c, rows=128):
                return stats[0:rows, slot * 16 + c: slot * 16 + c + 1]

            def emit_A(P):
                R = {n: Res() for n in (
                    "xbuf0", "xbuf1", "xc0", "xc1", "xn2b0", "xn2b1",
                    "xnb0", "xnb1", "r320", "r321", "xn2tP",
                    "xntP0", "xntP1", "rtP0", "rtP1", "qtP0", "qtP1",
                    "kvKP0", "kvKP1", "kvVP0", "kvVP1",
                    "ebP0", "ebP1", "ptbP0", "ptbP1", "PV0", "PV1",
                    "dgb0", "dgb1", "dgb2", "dgb3", "otP0", "otP1",
                    "routP0", "routP1", "PS0", "PS1", "PP0", "PP1",
                    "PO0", "PO1", "PTb0", "wqkv_b",
                    "stats0", "stats1", "stats2", "stats3", "stats4",
                    "negmx0", "negmx1", "negmx2", "negmx3",
                    "ssum0", "ssum1", "ssum2", "ssum3",
                    "srec0", "srec1", "srec2", "srec3",
                    "ev1zb", "ev2zb", "oddtb",
                )}

                R["PV1"] = R["PV0"]

                def res(base, i=None):
                    return R[base + ("" if i is None else str(i))]

                P.dma("sy", "swa", gm_sb[:], gmd.ap())
                P.dma("sy", "swa", b1_sb[:], b1md.ap())
                P.dma("sy", "swa", b2_sb[:], b2md.ap())
                P.dma("sy", "swa", idf[:], identd.ap())
                P.dma("sy", "swa", xbuf0[0:8, :], xled.ap())
                P.dma("sy", "swa", xbuf1[0:8, :], xlod.ap())
                P.dma("sy", "swa", r320[:], xhd.ap())
                commit_dma(P, "swa", "pload",
                           writes=[R["xbuf0"], R["xbuf1"], R["r320"]])

                P.dma("gp", "swg", idb[:], identd.ap())
                P.dma("gp", "swg", m0b[:], m0d.ap())
                P.dma("gp", "swg", mrb[:], mrd.ap())
                P.dma("gp", "swg", wo_b[:],
                      wod.ap().rearrange("(a b) c -> b a c", b=128))
                P.dma("gp", "swg", sc8[:], bass.AP(scd, 0, [[0, 8], [1, 1]]))
                commit_dma(P, "swg", "pcast", writes=[])

                for dcc in range(DC):
                    for hf_ in range(2):
                        acquire(P, "sy", writes=[R["rtP0"]])
                        P.dma("sy", "sbi0", rtP0[:, 0:1536],
                              wqkvd.ap()[dcc * 128:(dcc + 1) * 128,
                                         hf_ * 1536:(hf_ + 1) * 1536])
                        commit_dma(P, "sbi0", f"wql{dcc}_{hf_}",
                                   writes=[R["rtP0"]])
                        acquire(P, "ve", reads=[R["rtP0"]], writes=[R["wqkv_b"]])
                        P.wait_ck("ve", "pload")
                        if hf_ == 0:
                            P.op("ve", "tensor_scalar",
                                 wqkv_b[:, dcc * 3072: dcc * 3072 + 1024],
                                 rtP0[:, 0:1024], gm_sb[:, dcc:dcc + 1], 0.125,
                                 ALU.mult, ALU.mult)
                            P.op("ve", "tensor_scalar",
                                 wqkv_b[:, dcc * 3072 + 1024:
                                        dcc * 3072 + 1536],
                                 rtP0[:, 1024:1536], gm_sb[:, dcc:dcc + 1],
                                 None, ALU.mult, inc=True)
                        else:
                            P.op("ve", "tensor_scalar",
                                 wqkv_b[:, dcc * 3072 + 1536:
                                        dcc * 3072 + 3072],
                                 rtP0[:, 0:1536], gm_sb[:, dcc:dcc + 1],
                                 None, ALU.mult, inc=True)
                        commit(P, "ve", f"wqc{dcc}_{hf_}", reads=[R["rtP0"]],
                               writes=[R["wqkv_b"]])

                for li, (src, dst, xres, dres) in enumerate(
                        ((xbuf0, xc0, R["xbuf0"], R["xc0"]),
                         (xbuf1, xc1, R["xbuf1"], R["xc1"]))):
                    sl = 3
                    cb = 32 * li
                    def lc(c):
                        return stats[0:8, 48 + li * 16 + c: 48 + li * 16 + c + 1]
                    sres = R["stats" + str(3 + li)]
                    sqi = (xnb0, xnb1)[li]
                    sqs = (xn2b0, xn2b1)[li]
                    sqir = res("xnb", li)
                    sqsr = res("xn2b", li)
                    acquire(P, "se", reads=[xres], writes=[sqir, sqsr, sres])
                    P.op("se", "activation", sqi[0:8, :], src[0:8, :],
                         AF.Identity, accum_out=lc(0))
                    P.op("se", "activation", sqs[0:8, :], src[0:8, :],
                         AF.Square, accum_out=lc(3), inc=True)
                    commit(P, "se", f"lsq{li}", reads=[xres],
                           writes=[sqir, sqsr])

                    acquire(P, "ve", writes=[sres])
                    P.wait_ck("ve", f"lsq{li}")
                    P.op("ve", "tensor_scalar", lc(1), lc(0), 1.0 / D, None, ALU.mult)
                    P.op("ve", "tensor_scalar", lc(4), lc(3), 1.0 / D, None, ALU.mult)
                    P.op("ve", "tensor_scalar", lc(15), lc(0), lc(0),
                         1.0 / (D * D), ALU.mult, ALU.mult)
                    P.op("ve", "tensor_scalar", lc(2), lc(0), -1.0 / D, None,
                         ALU.mult)
                    P.op("ve", "tensor_tensor", lc(5), lc(4), lc(15), ALU.subtract)
                    P.op("ve", "tensor_scalar", lc(14), lc(5), 1e-5, None, ALU.add,
                         inc=True)
                    commit(P, "ve", f"lst{li}", writes=[sres])

                    P.wait_ck("se", f"lst{li}")
                    P.op("se", "activation", lc(12), lc(14), AF.Ln)
                    P.op("se", "activation", lc(6), lc(12), AF.Exp, scale=-0.5,
                         inc=True)
                    commit(P, "se", f"lrs{li}", writes=[sres])
                    P.wait_ck("ve", f"lrs{li}")
                    P.op("ve", "tensor_tensor", lc(15), lc(6), lc(6), ALU.mult)
                    P.op("ve", "tensor_tensor", lc(13), lc(15), lc(14), ALU.mult)
                    P.op("ve", "tensor_scalar", lc(12), lc(13), -0.5, 1.5,
                         ALU.mult, ALU.add)
                    P.op("ve", "tensor_tensor", lc(7), lc(6), lc(12), ALU.mult)
                    P.op("ve", "tensor_tensor", lc(8), lc(2), lc(7), ALU.mult)
                    commit(P, "ve", f"lnm{li}", writes=[sres])
                    acquire(P, "ve", reads=[xres], writes=[dres])
                    P.op("ve", "tensor_scalar", dst[0:8, :], src[0:8, :],
                         lc(7), lc(8), ALU.mult, ALU.add, inc=True)
                    commit(P, "ve", f"lxn{li}", reads=[xres, sres],
                           writes=[dres])

                acquire(P, "ve", writes=[R["ev1zb"]])
                P.op("ve", "memset", ev1zb[:], 0.0, inc=True)
                commit(P, "ve", "evz1", writes=[R["ev1zb"]])
                acquire(P, "gp", reads=[R["xc0"]], writes=[R["ev1zb"]])
                P.wait_ck("gp", "lxn0")
                for i in range(4):
                    P.dma("gp", "slt", ev1zb[i:i + 1, i * 1024:(i + 1) * 1024],
                          xc0.ap()[i:i + 1, 0:1024])

                acquire(P, "ve", writes=[R["ev2zb"]])
                P.op("ve", "memset", ev2zb[:], 0.0, inc=True)
                commit(P, "ve", "evz2", writes=[R["ev2zb"]])
                acquire(P, "gp", reads=[R["xc0"]], writes=[R["ev2zb"]])
                for i in range(4):
                    for r in range(4):
                        P.dma("gp", "slt",
                              ev2zb[i:i + 1,
                                    r * 1024 + i * 256: r * 1024 + i * 256 + 256],
                              xc0.ap()[4 + i:5 + i, r:1024:4],
                              allow_slow_non_contiguous=True)
                commit_dma(P, "slt", "evd2",
                           reads=[R["xc0"]],
                           writes=[R["ev1zb"], R["ev2zb"]])

                acquire(P, "te", reads=[R["xc1"]], writes=[R["PS0"]])
                P.wait_ck("te", "lxn1")
                P.wait_ck("te", "pload")
                for dcc in range(DC):
                    P.op("te", "transpose",
                         PS0[:, dcc * 8: dcc * 8 + 8],
                         xc1[0:8, dcc * 128:(dcc + 1) * 128],
                         idf[0:8, 0:8], inc=(dcc == DC - 1))
                commit(P, "te", "oddt", reads=[R["xc1"]], writes=[R["PS0"]])
                acquire(P, "ve", reads=[R["PS0"]], writes=[R["oddtb"]])
                P.wait_ck("ve", "oddt")
                P.op("ve", "tensor_copy", oddtb[:], PS0[:, 0:64], inc=True)
                commit(P, "ve", "oddte", reads=[R["PS0"]], writes=[R["oddtb"]])

                def emit_prep(b):
                    for _ in emit_prep_gen(b):
                        pass

                def emit_prep_gen(b):
                    halo = b < 0
                    p = b % 2 if not halo else 1
                    slot = 2 if halo else p
                    xs = r320 if halo else xbuf[p]
                    xsr = R["r320"] if halo else res("xbuf", p)
                    xcs = xc1 if halo else xcb[p]
                    xcr = R["xc1"] if halo else res("xc", p)
                    x2s = xn2b1 if halo else xn2b[p]
                    x2r = R["xn2b1"] if halo else res("xn2b", p)
                    tag = "h" if halo else str(b)

                    def sc_(c):
                        return scol(slot, c)

                    if not halo:
                        acquire(P, "sy", writes=[xsr])
                        P.dma("sy", f"sx{p}", xs[:],
                              xd.ap()[b * 128:(b + 1) * 128, :])
                        commit_dma(P, f"sx{p}", f"ldx{b}", writes=[xsr])
                    yield None

                    acquire(P, "se", reads=[xsr],
                            writes=[res("stats", slot), x2r])
                    P.op("se", "activation", x2s[:], xs[:], AF.Square,
                         accum_out=sc_(3), inc=True)
                    commit(P, "se", f"sq{tag}", reads=[xsr],
                           writes=[x2r])
                    yield None

                    acquire(P, "ve", reads=[xsr], writes=[res("stats", slot)])
                    P.op("ve", "reduce_sum", sc_(0), xs[:], AX)
                    P.op("ve", "tensor_scalar", sc_(1), sc_(0), 1.0 / D, None, ALU.mult)
                    P.op("ve", "tensor_scalar", sc_(15), sc_(0), sc_(0),
                         1.0 / (D * D), ALU.mult, ALU.mult)
                    P.op("ve", "tensor_scalar", sc_(2), sc_(0), -1.0 / D, None,
                         ALU.mult)
                    P.wait_ck("ve", f"sq{tag}")
                    P.op("ve", "tensor_scalar", sc_(4), sc_(3), 1.0 / D, None, ALU.mult)
                    P.op("ve", "tensor_tensor", sc_(5), sc_(4), sc_(15),
                         ALU.subtract)
                    P.op("ve", "tensor_scalar", sc_(14), sc_(5), 1e-5, None,
                         ALU.add, inc=True)
                    commit(P, "ve", f"st{tag}",
                           reads=[xsr], writes=[res("stats", slot)])
                    yield None

                    P.wait_ck("se", f"st{tag}")
                    P.op("se", "activation", sc_(12), sc_(14), AF.Ln)
                    P.op("se", "activation", sc_(6), sc_(12), AF.Exp, scale=-0.5,
                         inc=True)
                    commit(P, "se", f"ch_{tag}", writes=[res("stats", slot)])
                    yield None

                    acquire(P, "ve", reads=[xsr], writes=[xcr])
                    P.wait_ck("ve", f"ch_{tag}")
                    P.op("ve", "tensor_tensor", sc_(15), sc_(6), sc_(6), ALU.mult)
                    P.op("ve", "tensor_tensor", sc_(13), sc_(15), sc_(14), ALU.mult)
                    P.op("ve", "tensor_scalar", sc_(12), sc_(13), -0.5, 1.5,
                         ALU.mult, ALU.add)
                    P.op("ve", "tensor_tensor", sc_(7), sc_(6), sc_(12), ALU.mult)
                    P.op("ve", "tensor_tensor", sc_(3), sc_(7), sc_(7), ALU.mult)
                    P.op("ve", "tensor_scalar", sc_(11), sc_(7), 1.0, None,
                         ALU.add)
                    P.op("ve", "tensor_scalar", sc_(8), sc_(5), sc_(3), 1e-6,
                         ALU.mult, ALU.add)
                    P.op("ve", "tensor_scalar", xcs[:], xs[:], sc_(2), None,
                         ALU.add, inc=True)
                    commit(P, "ve", f"ch2_{tag}", reads=[xsr],
                           writes=[res("stats", slot), xcr])
                    yield None

                    P.wait_ck("se", f"ch2_{tag}")
                    P.op("se", "activation", sc_(12), sc_(8), AF.Ln)
                    P.op("se", "activation", sc_(9), sc_(12), AF.Exp, scale=-0.5,
                         inc=True)
                    commit(P, "se", f"ch3_{tag}", writes=[res("stats", slot)])
                    yield None

                    acquire(P, "ve", reads=[xcr], writes=[x2r])
                    P.wait_ck("ve", f"ch3_{tag}")
                    P.op("ve", "tensor_tensor", sc_(0), sc_(9), sc_(9), ALU.mult)
                    P.op("ve", "tensor_tensor", sc_(13), sc_(0), sc_(8), ALU.mult)
                    P.op("ve", "tensor_scalar", sc_(4), sc_(13), -0.5, 1.5,
                         ALU.mult, ALU.add)
                    P.op("ve", "tensor_tensor", sc_(2), sc_(9), sc_(4), ALU.mult)
                    P.op("ve", "tensor_tensor", sc_(10), sc_(7), sc_(2), ALU.mult)
                    P.op("ve", "tensor_scalar", x2s[:], xcs[:], sc_(10), None,
                         ALU.mult, inc=True)
                    commit(P, "ve", f"x2{tag}",
                           reads=[xcr, res("stats", slot)], writes=[x2r])
                    yield None

                    if not halo:
                        acquire(P, "ve", reads=[xcr],
                                writes=[res("xnb", p), res("r32", p)])
                        P.op("ve", "tensor_scalar", xnb[p][:], xcs[:], sc_(7),
                             None, ALU.mult)
                        P.op("ve", "tensor_scalar", r32[p][:], xcs[:], sc_(11),
                             sc_(1), ALU.mult, ALU.add, inc=True)
                        commit(P, "ve", f"xnr{tag}",
                               reads=[xcr, res("stats", slot)],
                               writes=[res("xnb", p), res("r32", p)])


                NPAIR = NBLK // 2

                def emit_tp(tb, bank, bankres, src_t, srcres, dst_t, dstres,
                            off, ident, tag):
                    acquire(P, "te", reads=[srcres], writes=[bankres])
                    P.wait_ck("te", "pcast")
                    for dcc in range(DC):
                        P.op("te", "transpose",
                             bank[:, dcc * 128:(dcc + 1) * 128],
                             src_t[:, dcc * 128:(dcc + 1) * 128], ident,
                             inc=(dcc == DC - 1))
                    commit(P, "te", f"tp{tag}", reads=[srcres],
                           writes=[bankres])
                    acquire(P, "ve", reads=[bankres], writes=[dstres])
                    P.wait_ck("ve", f"tp{tag}")
                    P.op("ve", "tensor_copy",
                         bass.AP(dst_t, off, [[2048, 128], [256, 8], [1, 128]]),
                         bass.AP(bank, 0, [[1024, 128], [128, 8], [1, 128]]),
                         inc=True)
                    commit(P, "ve", f"tpe{tag}", reads=[bankres],
                           writes=[dstres])

                def emit_rt(b, half, tag):
                    p = b % 2
                    qj = (b // 2) % 2
                    bres = res("PO", half)
                    acquire(P, "te", reads=[res("r32", p)], writes=[bres])
                    for di in range(4):
                        dcc = half * 4 + di
                        P.op("te", "transpose",
                             PO[half][:, di * 128:(di + 1) * 128],
                             r32[p][:, dcc * 128:(dcc + 1) * 128], idf[:],
                             inc=(di == 3))
                    commit(P, "te", f"trt{tag}", reads=[res("r32", p)],
                           writes=[bres])
                    acquire(P, "ve", reads=[bres], writes=[res("rtP", qj)])
                    P.wait_ck("ve", f"trt{tag}")
                    P.op("ve", "tensor_copy",
                         bass.AP(rtP[qj], half * 4 * 256 + p * 128,
                                 [[2048, 128], [256, 4], [1, 128]]),
                         bass.AP(PO[half], 0, [[512, 128], [128, 4], [1, 128]]),
                         inc=True)
                    commit(P, "ve", f"ert{tag}", reads=[bres],
                           writes=[res("rtP", qj)])

                def emit_pre_pair(j):
                    q = j % 2
                    B0, B1 = 2 * j, 2 * j + 1
                    emit_tp(j, PTb0, R["PTb0"], xn2b0, R["xn2b0"],
                            xn2tP, R["xn2tP"], 0, idb[:], f"x2_{B0}")
                    yield None
                    emit_tp(j, PTb0, R["PTb0"], xn2b1, R["xn2b1"],
                            xn2tP, R["xn2tP"], 128, idb[:], f"x2_{B1}")
                    yield None
                    for L in range(8):
                        m0 = 2 * L
                        bank = L % 2
                        bres = res("PO", bank)
                        acquire(P, "te", reads=[R["xn2tP"], R["wqkv_b"]],
                                writes=[bres])
                        for mi in range(2):
                            m = m0 + mi
                            for dcc in range(DC):
                                last = (mi == 1 and dcc == DC - 1)
                                P.op("te", "matmul",
                                     PO[bank][:, mi * 256:(mi + 1) * 256],
                                     wqkv_b[:, dcc * 3072 + m * 128:
                                            dcc * 3072 + m * 128 + 128],
                                     xn2tP[:, dcc * 256:(dcc + 1) * 256],
                                     start=(mi == 0 and dcc == 0),
                                     stop=last, inc=last)
                        commit(P, "te", f"qk{j}_{L}", reads=[R["xn2tP"]],
                               writes=[bres])
                        if L < 4:
                            tres = res("qtP", q)
                            acquire(P, "ve", reads=[bres], writes=[tres])
                            P.wait_ck("ve", f"qk{j}_{L}")
                            P.op("ve", "tensor_copy",
                                 qtP[q][:, L * 512:(L + 1) * 512],
                                 PO[bank][:, 0:512], inc=True)
                            commit(P, "ve", f"eqk{j}_{L}", reads=[bres],
                                   writes=[tres])
                        else:
                            tres = res("kvKP", q)
                            acquire(P, "ve", reads=[bres], writes=[tres])
                            P.wait_ck("ve", f"qk{j}_{L}")
                            P.op("ve", "tensor_copy",
                                 bass.AP(kvKP[q], (L - 4) * 2 * 384 + 128,
                                         [[3072, 128], [384, 2], [1, 256]]),
                                 bass.AP(PO[bank], 0,
                                         [[512, 128], [256, 2], [1, 256]]),
                                 inc=True)
                            commit(P, "ve", f"eqk{j}_{L}", reads=[bres],
                                   writes=[tres])
                        yield None
                    for blk in range(2):
                        for rnd in range(2):
                            bank = rnd
                            bres = res("PO", bank)
                            acquire(P, "te", reads=[R["xn2tP"], R["wqkv_b"]],
                                    writes=[bres])
                            for dcc in range(DC):
                                P.op("te", "matmul", PO[bank][:, 0:512],
                                     xn2tP[:, dcc * 256 + blk * 128:
                                           dcc * 256 + blk * 128 + 128],
                                     wqkv_b[:, dcc * 3072 + 2048 + rnd * 512:
                                            dcc * 3072 + 2048 + rnd * 512 + 512],
                                     start=(dcc == 0), stop=(dcc == DC - 1),
                                     inc=(dcc == DC - 1))
                            commit(P, "te", f"vm{j}_{blk}_{rnd}",
                                   reads=[R["xn2tP"]], writes=[bres])
                            tres = res("kvVP", q)
                            acquire(P, "ve", reads=[bres], writes=[tres])
                            P.wait_ck("ve", f"vm{j}_{blk}_{rnd}")
                            P.op("ve", "tensor_copy",
                                 kvVP[q][:, blk * 1024 + rnd * 512:
                                         blk * 1024 + rnd * 512 + 512],
                                 PO[bank][:, 0:512], inc=True)
                            commit(P, "ve", f"evv{j}_{blk}_{rnd}",
                                   reads=[bres], writes=[tres])
                            yield None
                    emit_tp(j, PTb0, R["PTb0"], xnb0, R["xnb0"],
                            xntP[q], res("xntP", q), 0, idb[:], f"xn_{B0}")
                    yield None
                    emit_tp(j, PTb0, R["PTb0"], xnb1, R["xnb1"],
                            xntP[q], res("xntP", q), 128, idb[:], f"xn_{B1}")
                    yield None
                    for blk in range(2):
                        for half in range(2):
                            emit_rt(2 * j + blk, half, f"{2 * j + blk}_{half}")
                            yield None

                def emit_halo_pre():
                    acquire(P, "te", reads=[R["xn2b1"]], writes=[R["PTb0"]])
                    P.wait_ck("te", "pcast")
                    for dcc in range(DC):
                        P.op("te", "transpose",
                             PTb0[:, dcc * 128:(dcc + 1) * 128],
                             xn2b1[:, dcc * 128:(dcc + 1) * 128], idb[:],
                             inc=(dcc == DC - 1))
                    commit(P, "te", "tph", reads=[R["xn2b1"]],
                           writes=[R["PTb0"]])
                    acquire(P, "ve", reads=[R["PTb0"]], writes=[R["xn2tP"]])
                    P.wait_ck("ve", "tph")
                    P.op("ve", "tensor_copy",
                         bass.AP(xn2tP, 128, [[2048, 128], [256, 8], [1, 128]]),
                         bass.AP(PTb0, 0, [[1024, 128], [128, 8], [1, 128]]),
                         inc=True)
                    commit(P, "ve", "tpeh", reads=[R["PTb0"]],
                           writes=[R["xn2tP"]])
                    for L in range(2):
                        bank = L % 2
                        bres = res("PO", bank)
                        acquire(P, "te", reads=[R["xn2tP"], R["wqkv_b"]],
                                writes=[bres])
                        for mi in range(4):
                            m = 8 + L * 4 + mi
                            for dcc in range(DC):
                                last = (mi == 3 and dcc == DC - 1)
                                P.op("te", "matmul",
                                     PO[bank][:, mi * 128:(mi + 1) * 128],
                                     wqkv_b[:, dcc * 3072 + m * 128:
                                            dcc * 3072 + m * 128 + 128],
                                     xn2tP[:, dcc * 256 + 128:
                                           dcc * 256 + 256],
                                     start=(mi == 0 and dcc == 0),
                                     stop=last, inc=last)
                        commit(P, "te", f"qkh_{L}", reads=[R["xn2tP"]],
                               writes=[bres])
                        acquire(P, "ve", reads=[bres], writes=[res("kvKP", 0)])
                        P.wait_ck("ve", f"qkh_{L}")
                        P.op("ve", "tensor_copy",
                             bass.AP(kvKP[0], L * 4 * 384,
                                     [[3072, 128], [384, 4], [1, 128]]),
                             bass.AP(PO[bank], 0,
                                     [[512, 128], [128, 4], [1, 128]]),
                             inc=True)
                        commit(P, "ve", f"eqkh_{L}", reads=[bres],
                               writes=[res("kvKP", 0)])
                    for rnd in range(2):
                        bank = rnd
                        bres = res("PO", bank)
                        acquire(P, "te", reads=[R["xn2tP"], R["wqkv_b"]],
                                writes=[bres])
                        for dcc in range(DC):
                            P.op("te", "matmul", PO[bank][:, 0:512],
                                 xn2tP[:, dcc * 256 + 128: dcc * 256 + 256],
                                 wqkv_b[:, dcc * 3072 + 2048 + rnd * 512:
                                        dcc * 3072 + 2048 + rnd * 512 + 512],
                                 start=(dcc == 0), stop=(dcc == DC - 1),
                                 inc=(dcc == DC - 1))
                        commit(P, "te", f"vmh_{rnd}", reads=[R["xn2tP"]],
                               writes=[bres])
                        acquire(P, "ve", reads=[bres], writes=[res("kvVP", 1)])
                        P.wait_ck("ve", f"vmh_{rnd}")
                        P.op("ve", "tensor_copy",
                             kvVP[1][:, 1024 + rnd * 512:
                                     1024 + rnd * 512 + 512],
                             PO[bank][:, 0:512], inc=True)
                        commit(P, "ve", f"evvh_{rnd}", reads=[bres],
                               writes=[res("kvVP", 1)])

                def emit_heads(b, feeder, cap, fstart=0):
                    q = (b // 2) % 2
                    p = b % 2
                    mask = m0b if b == 0 else mrb
                    NPH = NH // 2

                    def st_scores(g):
                        gb = g % 2
                        pr = (2 * g) % 4
                        sb = res("PS", gb)
                        acquire(P, "te", reads=[res("qtP", q), res("kvKP", q)],
                                writes=[sb])
                        if g == 0:
                            P.wait_ck("te", "pcast")
                        for i in range(2):
                            h = 2 * g + i
                            po = i * 64
                            P.op("te", "matmul",
                                 PS[gb][:, i * 256:(i + 1) * 256],
                                 idb[:], mask[:, i * 256:(i + 1) * 256],
                                 start=True, stop=False)
                            P.op("te", "matmul",
                                 PS[gb][:, i * 256:(i + 1) * 256],
                                 qtP[q][po:po + 64,
                                        g * 256 + p * 128: g * 256 + p * 128 + 128],
                                 kvKP[q][po:po + 64,
                                         g * 384 + p * 128: g * 384 + p * 128 + 256],
                                 start=False, stop=True, inc=(i == 1))
                        commit(P, "te", f"sc{b}_{g}",
                               reads=[res("qtP", q), res("kvKP", q)],
                               writes=[sb])

                        acquire(P, "ve", reads=[sb], writes=[res("negmx", pr)])
                        P.wait_ck("ve", f"sc{b}_{g}")
                        P.op("ve", "reduce_max", negmx[:, pr:pr + 2],
                             bass.AP(PS[gb], 0, [[512, 128], [256, 2], [1, 256]]),
                             AX, negate=True, inc=True)
                        commit(P, "ve", f"mx{b}_{g}", reads=[sb],
                               writes=[res("negmx", pr)])

                        acquire(P, "se", reads=[sb, res("negmx", pr)],
                                writes=[res("ebP", gb), res("ssum", pr)])
                        for i in range(2):
                            P.op("se", "activation",
                                 ebP[gb][:, i * 256:(i + 1) * 256],
                                 PS[gb][:, i * 256:(i + 1) * 256],
                                 AF.Exp, bias=negmx[:, pr + i:pr + i + 1],
                                 accum_out=ssum[:, pr + i:pr + i + 1],
                                 inc=(i == 1))
                        commit(P, "se", f"ex{b}_{g}", reads=[sb],
                               writes=[res("ebP", gb), res("ssum", pr)])

                    def st_dg(g):
                        pr = (2 * g) % 4
                        acquire(P, "ve", reads=[res("ssum", pr)],
                                writes=[res("srec", pr), res("dgb", pr),
                                        res("dgb", pr + 1)])
                        P.wait_ck("ve", f"ex{b}_{g}")
                        P.op("ve", "reciprocal", srec[:, pr:pr + 2],
                             ssum[:, pr:pr + 2])
                        P.op("ve", "tensor_scalar", dgb[pr][:], idb[:],
                             srec[:, pr:pr + 1], None, ALU.mult)
                        P.op("ve", "tensor_scalar", dgb[pr + 1][:], idb[:],
                             srec[:, pr + 1:pr + 2], None, ALU.mult, inc=True)
                        commit(P, "ve", f"dg{b}_{g}",
                               reads=[res("ssum", pr)],
                               writes=[res("srec", pr), res("dgb", pr),
                                       res("dgb", pr + 1)])

                    def st_pt(g):
                        gb = g % 2
                        pr = (2 * g) % 4
                        pb_ = res("PP", gb)
                        acquire(P, "te", reads=[res("ebP", gb), res("dgb", pr),
                                                res("dgb", pr + 1)],
                                writes=[pb_])
                        for i in range(2):
                            P.op("te", "matmul",
                                 PP[gb][:, i * 256: i * 256 + 128],
                                 ebP[gb][:, i * 256: i * 256 + 128],
                                 dgb[pr + i][:], start=(i == 0), stop=False)
                            P.op("te", "matmul",
                                 PP[gb][:, i * 256 + 128: i * 256 + 256],
                                 ebP[gb][:, i * 256 + 128: i * 256 + 256],
                                 dgb[pr + i][:], start=False, stop=(i == 1),
                                 inc=(i == 1))
                        commit(P, "te", f"pt{b}_{g}",
                               reads=[res("ebP", gb), res("dgb", pr),
                                      res("dgb", pr + 1)],
                               writes=[pb_])

                        acquire(P, "se", reads=[pb_], writes=[res("ptbP", gb)])
                        P.wait_ck("se", f"pt{b}_{g}")
                        P.op("se", "activation", ptbP[gb][:], PP[gb][:, 0:512],
                             AF.Copy, inc=True)
                        commit(P, "se", f"pte{b}_{g}", reads=[pb_],
                               writes=[res("ptbP", gb)])

                    def st_pv(g):
                        gb = g % 2
                        pb_ = res("PV", gb)
                        if p == 0:
                            vprev_t = kvVP[1 - q]
                            vprev_res = res("kvVP", 1 - q)
                            vprev_off = 1024
                        else:
                            vprev_t = kvVP[q]
                            vprev_res = res("kvVP", q)
                            vprev_off = 0
                        acquire(P, "te", reads=[res("kvVP", q), vprev_res,
                                                res("ptbP", gb)],
                                writes=[pb_])
                        for i in range(2):
                            h = 2 * g + i
                            po = i * 64
                            tp = (0, po) if po else None
                            P.op("te", "matmul", PV[gb][po:po + 64, 0:128],
                                 vprev_t[:, vprev_off + h * 64:
                                         vprev_off + h * 64 + 64],
                                 ptbP[gb][:, i * 256: i * 256 + 128],
                                 start=True, stop=False, tile_position=tp)
                            P.op("te", "matmul", PV[gb][po:po + 64, 0:128],
                                 kvVP[q][:, p * 1024 + h * 64:
                                         p * 1024 + h * 64 + 64],
                                 ptbP[gb][:, i * 256 + 128: i * 256 + 256],
                                 start=False, stop=True, tile_position=tp,
                                 inc=(i == 1))
                        commit(P, "te", f"pv{b}_{g}",
                               reads=[res("kvVP", q), vprev_res,
                                      res("ptbP", gb)],
                               writes=[pb_])

                        acquire(P, "se", reads=[pb_], writes=[res("otP", q)])
                        P.wait_ck("se", f"pv{b}_{g}")
                        P.op("se", "activation",
                             otP[q][:, g * 256 + p * 128: g * 256 + p * 128 + 128],
                             PV[gb][:, 0:128], AF.Copy, inc=True)
                        commit(P, "se", f"oe{b}_{g}", reads=[pb_],
                               writes=[res("otP", q)])

                    for it in range(NPH + 3):
                        if 2 <= it <= NPH + 1:
                            st_pt(it - 2)
                        if it >= 3:
                            st_pv(it - 3)
                        if it < NPH:
                            st_scores(it)
                        if 1 <= it <= NPH:
                            st_dg(it - 1)
                        if it >= fstart:
                            feeder.feed(cap)


                def emit_wo_stores(j):
                    q = j % 2
                    for L in range(4):
                        wb_ = res("PP", L % 2)
                        acquire(P, "te", reads=[res("otP", q)], writes=[wb_])
                        if L == 0:
                            P.wait_ck("te", "pcast")
                        for di in range(2):
                            dcc = 2 * L + di
                            for oc in range(DC):
                                last = (di == 1 and oc == DC - 1)
                                P.op("te", "matmul",
                                     PP[L % 2][:, di * 256:(di + 1) * 256],
                                     wo_b[:, oc * 1024 + dcc * 128:
                                          oc * 1024 + dcc * 128 + 128],
                                     otP[q][:, oc * 256:(oc + 1) * 256],
                                     start=(di == 0 and oc == 0),
                                     stop=last, inc=last)
                        commit(P, "te", f"wo{j}_{L}", reads=[res("otP", q)],
                               writes=[wb_])

                        acquire(P, "ve", reads=[wb_, res("rtP", q)],
                                writes=[res("routP", q)])
                        P.wait_ck("ve", f"wo{j}_{L}")
                        P.op("ve", "scalar_tensor_tensor",
                             routP[q][:, L * 512:(L + 1) * 512],
                             PP[L % 2][:, 0:512], 1.0,
                             rtP[q][:, L * 512:(L + 1) * 512],
                             ALU.mult, ALU.add, inc=True)
                        commit(P, "ve", f"ewo{j}_{L}",
                               reads=[wb_, res("rtP", q)], writes=[res("routP", q)])

                    acquire(P, "gp", reads=[res("routP", q), res("xntP", q)])
                    P.dma("gp", f"sst{j % 2}",
                          scr_r.ap().rearrange("(a b) c -> b a c", b=128)
                          [:, :, j * 256:(j + 1) * 256],
                          bass.AP(routP[q], 0, [[2048, 128], [256, 8], [1, 256]]))
                    P.dma("gp", f"sst{j % 2}",
                          scr_xn.ap().rearrange("(a b) c -> b a c", b=128)
                          [:, :, j * 256:(j + 1) * 256],
                          bass.AP(xntP[q], 0, [[2048, 128], [256, 8], [1, 256]]))
                    commit_dma(P, f"sst{j % 2}", f"str{j}",
                               reads=[res("routP", q), res("xntP", q)])

                class Feeder:
                    def __init__(self, gen):
                        self.gen = gen
                        self.count = 0
                        self.done = gen is None

                    def feed(self, cap):
                        if not self.done and self.count < cap:
                            try:
                                next(self.gen)
                                self.count += 1
                            except StopIteration:
                                self.done = True

                    def drain(self):
                        if not self.done:
                            for _ in self.gen:
                                pass
                            self.done = True

                emit_prep(-1)
                emit_prep(0)
                emit_halo_pre()
                emit_prep(1)
                Feeder(emit_pre_pair(0)).drain()
                for j in range(NPAIR):
                    gens = [emit_prep_gen(b2) for b2 in (2 * j + 2, 2 * j + 3)
                            if b2 < NBLK]
                    alive = True
                    while alive:
                        alive = False
                        for gg in gens:
                            try:
                                next(gg)
                                alive = True
                            except StopIteration:
                                pass
                    if j > 0:
                        emit_wo_stores(j - 1)
                    if j < NPAIR - 1:
                        q = j % 2
                        acquire(P, "ve", reads=[res("kvKP", q)],
                                writes=[res("kvKP", 1 - q)])
                        P.op("ve", "tensor_copy",
                             bass.AP(kvKP[1 - q], 0,
                                     [[3072, 128], [384, 8], [1, 128]]),
                             bass.AP(kvKP[q], 256,
                                     [[3072, 128], [384, 8], [1, 128]]),
                             inc=True)
                        commit(P, "ve", f"kvc{j}", reads=[res("kvKP", q)],
                               writes=[res("kvKP", 1 - q)])
                    feeder = Feeder(emit_pre_pair(j + 1)
                                    if j + 1 < NPAIR else None)
                    emit_heads(2 * j, feeder, 10)
                    emit_heads(2 * j + 1, feeder, 14)
                    feeder.drain()
                emit_wo_stores(NPAIR - 1)


                for e in ("te", "ve", "se"):
                    P.ckpt(e, f"Aend_{e}")
                if dbgst is not None:
                    for e in ("ve", "se"):
                        P.wait_ck("gp", f"Aend_{e}")
                    P.dma("gp", "slt", dbgst.ap(), stats[:])
                    commit_dma(P, "slt", "dbgstd")
                P.dma_ckpt("sst0", "Ast0")
                P.dma_ckpt("sst1", "Ast1")

            Pc = Prog("count", ck)
            emit_A(Pc)
            totals_A = (dict(Pc.cnt), dict(Pc.dcnt))
            track_A = ({e: {tn: dict(d) for tn, d in sh.items()}
                        for e, sh in Pc.shadow.items()}, dict(Pc.self_hw))

            with nc.Block() as blk:
                @blk.tensor
                def _(e):
                    Px = Prog("emit", ck, "te", {"te": e}, esem, dsem)
                    emit_A(Px)
                    assert Px.cnt == totals_A[0]

                @blk.vector
                def _(e):
                    Px = Prog("emit", ck, "ve", {"ve": e}, esem, dsem)
                    emit_A(Px)

                @blk.scalar
                def _(e):
                    Px = Prog("emit", ck, "se", {"se": e}, esem, dsem)
                    emit_A(Px)

                @blk.sync
                def _(e):
                    Px = Prog("emit", ck, "sy", {"sy": e}, esem, dsem)
                    emit_A(Px)

                @blk.gpsimd
                def _(e):
                    Px = Prog("emit", ck, "gp", {"gp": e}, esem, dsem)
                    emit_A(Px)
                    assert Px.dcnt == totals_A[1]

        with contextlib.ExitStack() as stB:
            def sbB(name, shape, dtype):
                return stB.enter_context(nc.sbuf_tensor(name, shape, dtype))

            def pbB(name, shape, dtype):
                return stB.enter_context(nc.psum_tensor(name, shape, dtype))

            w1t_b = sbB("w1t_b", [128, DC * HF], BF16)
            w2t_b = sbB("w2t_b", [128, HCN * D], BF16)
            xnt20 = sbB("xnt20", [128, DC * TB], BF16)
            xnt21 = sbB("xnt21", [128, DC * TB], BF16)
            rtb0 = sbB("rtb0", [128, DC * TB], F32)
            rtb1 = sbB("rtb1", [128, DC * TB], F32)
            hbf0 = sbB("hbf0", [128, TB], BF16)
            hbf1 = sbB("hbf1", [128, TB], BF16)
            u_bf = sbB("u_bf", [4, TB], BF16)
            t2_bf = sbB("t2_bf", [4, 4 * TB], BF16)
            outb0 = sbB("outb0", [128, DC * TB], F32)
            outb1 = sbB("outb1", [128, DC * TB], F32)
            H0 = pbB("H0", [128, 512], F32)
            H1 = pbB("H1", [128, 512], F32)
            T2a = pbB("T2a", [128, 512], F32)
            T2b = pbB("T2b", [128, 512], F32)
            OUT0 = pbB("OUT0", [128, 512], F32)
            OUT1 = pbB("OUT1", [128, 512], F32)
            OUT2 = pbB("OUT2", [128, 512], F32)
            OUT3 = pbB("OUT3", [128, 512], F32)

            xnt2 = (xnt20, xnt21)
            rtb = (rtb0, rtb1)
            hbf = (hbf0, hbf1)
            outb = (outb0, outb1)
            HB = (H0, H1)
            T2 = (T2a, T2b)
            OUT = (OUT0, OUT1, OUT2, OUT3)

            def emit_B(P):
                R = {n: Res() for n in (
                    "xnt20", "xnt21", "rtb0", "rtb1", "hbf0", "hbf1",
                    "u_bf", "t2_bf", "outb0", "outb1",
                    "H0", "H1", "T2a", "T2b", "OUT0", "OUT1", "OUT2", "OUT3",
                )}

                def res(base, i=None):
                    return R[base + ("" if i is None else str(i))]

                for e in ("te", "ve", "se", "sy", "gp"):
                    for o in ("te", "ve", "se"):
                        if o != e:
                            P.wait_ck(e, f"Aend_{o}")
                    for n in ("Ast0", "Ast1", "pload", "pcast",
                              "evd2", "oddte"):
                        P.wait_ck(e, n)

                def w1_chunk(c):
                    P.dma("gp", f"swA{c % 8}",
                          w1t_b[:, c * 1024:(c + 1) * 1024],
                          w1td.ap().rearrange("(a b) c -> b a c", b=128)
                          [:, :, c * 128:(c + 1) * 128])

                def w2_chunk(c):
                    P.dma("gp", f"swB{c % 8}",
                          w2t_b[:, c * 1024:(c + 1) * 1024],
                          w2td.ap().rearrange("(a b) c -> b a c", b=128)
                          [:, c:c + 1, :])

                for c in range(8):
                    w1_chunk(c)
                    w2_chunk(c)

                for s in range(NST):
                    q = s % 2
                    t0 = s * TB

                    acquire(P, "sy", writes=[res("xnt2", q), res("rtb", q)])
                    P.dma("sy", f"sbi{q}",
                          bass.AP(xnt2[q], 0, [[2048, 128], [256, 8], [1, 256]]),
                          scr_xn.ap().rearrange("(a b) c -> b a c", b=128)
                          [:, :, t0:t0 + TB])
                    P.dma("sy", f"sbi{q}",
                          bass.AP(rtb[q], 0, [[2048, 128], [256, 8], [1, 256]]),
                          scr_r.ap().rearrange("(a b) c -> b a c", b=128)
                          [:, :, t0:t0 + TB])
                    commit_dma(P, f"sbi{q}", f"bl{s}",
                               writes=[res("xnt2", q), res("rtb", q)])

                    acquire(P, "te", reads=[res("xnt2", q)], writes=[R["T2a"]])
                    P.wait_ck("te", f"bl{s}")
                    for dcc in range(DC):
                        P.op("te", "matmul", T2a[0:4, 0:256],
                             oddtb[:, dcc * 8: dcc * 8 + 4],
                             xnt2[q][:, dcc * 256:(dcc + 1) * 256],
                             start=(dcc == 0), stop=(dcc == DC - 1),
                             inc=(dcc == DC - 1))
                    commit(P, "te", f"umm{s}", reads=[res("xnt2", q)],
                           writes=[R["T2a"]])
                    acquire(P, "ve", reads=[R["T2a"]], writes=[R["u_bf"]])
                    P.wait_ck("ve", f"umm{s}")
                    P.op("ve", "tensor_scalar", u_bf[:], T2a[0:4, 0:256],
                         sc8[0:4, 0:1], None, ALU.mult, inc=True)
                    commit(P, "ve", f"uev{s}", reads=[R["T2a"]],
                           writes=[R["u_bf"]])

                    def emit_w1(hcc):
                            hp = hcc % 2
                            r = hcc // 8
                            qc2 = hcc % 8
                            hb_ = res("H", hp)
                            acquire(P, "te", reads=[res("xnt2", q), R["u_bf"]],
                                    writes=[hb_])
                            if s == 0:
                                P.wait_dma("te", f"swA{hcc % 8}",
                                           16 * (hcc // 8 + 1))
                            for dcc in range(DC):
                                P.op("te", "matmul", HB[hp][:, 0:256],
                                     w1t_b[:, hcc * 1024 + dcc * 128:
                                           hcc * 1024 + dcc * 128 + 128],
                                     xnt2[q][:, dcc * 256:(dcc + 1) * 256],
                                     start=(dcc == 0), stop=False)
                            P.op("te", "matmul", HB[hp][:, 0:256],
                                 ev1zb[0:4, hcc * 128:(hcc + 1) * 128],
                                 u_bf[0:4, :], start=False, stop=True, inc=True)
                            commit(P, "te", f"w1{s}_{hcc}",
                                   reads=[res("xnt2", q), R["u_bf"]], writes=[hb_])
                            if s == 0 and hcc + 8 < HCN:
                                P.wait_ck("gp", f"w1{s}_{hcc}")
                                w1_chunk(hcc + 8)


                    def emit_gt(hcc):
                            hp = hcc % 2
                            r = hcc // 8
                            qc2 = hcc % 8
                            hb_ = res("H", hp)
                            acquire(P, "se", reads=[hb_], writes=[res("hbf", hp)])
                            P.wait_ck("se", f"w1{s}_{hcc}")
                            P.op("se", "activation", hbf[hp][:], HB[hp][:, 0:256],
                                 AF.Gelu, bias=b1_sb[:, hcc:hcc + 1], inc=True)
                            commit(P, "se", f"ge{s}_{hcc}", reads=[hb_],
                                   writes=[res("hbf", hp)])

                            t2bank = T2[r // 2]
                            t2res = res("T2", ("a", "b")[r // 2])
                            acquire(P, "te", reads=[res("hbf", hp)],
                                    writes=[t2res] + [res("OUT", i) for i in range(4)])
                            P.wait_ck("te", f"ge{s}_{hcc}")
                            if s == 0:
                                P.wait_dma("te", f"swB{hcc % 8}",
                                           16 * (hcc // 8 + 1))
                            P.op("te", "matmul",
                                 t2bank[0:4, (r % 2) * 256:(r % 2) * 256 + 256],
                                 oddtb[:, qc2 * 8 + 4: qc2 * 8 + 8],
                                 hbf[hp][:, 0:256],
                                 start=(qc2 == 0 and r % 2 == 0),
                                 stop=(qc2 == 7 and r % 2 == 1))
                            for oc in range(DC):
                                last = (oc == DC - 1)
                                P.op("te", "matmul",
                                     OUT[oc // 2][:, (oc % 2) * 256:(oc % 2) * 256 + 256],
                                     w2t_b[:, hcc * 1024 + oc * 128:
                                           hcc * 1024 + oc * 128 + 128],
                                     hbf[hp][:, 0:256],
                                     start=(hcc == 0 and oc % 2 == 0),
                                     stop=False, inc=last)
                            commit(P, "te", f"t2w2{s}_{hcc}",
                                   reads=[res("hbf", hp)],
                                   writes=[t2res] + [res("OUT", i) for i in range(4)])
                            if s == 0 and hcc + 8 < HCN:
                                P.wait_ck("gp", f"t2w2{s}_{hcc}")
                                w2_chunk(hcc + 8)

                    emit_w1(0)
                    for hcc in range(HCN):
                        if hcc + 1 < HCN:
                            emit_w1(hcc + 1)
                        emit_gt(hcc)

                    acquire(P, "ve", reads=[R["T2a"], R["T2b"]],
                            writes=[R["t2_bf"]])
                    P.wait_ck("ve", f"t2w2{s}_{HCN - 1}")
                    P.op("ve", "tensor_scalar", t2_bf[0:4, 0:512],
                         T2a[0:4, 0:512], sc8[0:4, 0:1], None, ALU.mult)
                    P.op("ve", "tensor_scalar", t2_bf[0:4, 512:1024],
                         T2b[0:4, 0:512], sc8[0:4, 0:1], None, ALU.mult,
                         inc=True)
                    commit(P, "ve", f"t2e{s}", reads=[R["T2a"], R["T2b"]],
                           writes=[R["t2_bf"]])

                    acquire(P, "te", reads=[R["t2_bf"]],
                            writes=[res("OUT", i) for i in range(4)])
                    P.wait_ck("te", f"t2e{s}")
                    for oc in range(DC):
                        for r in range(4):
                            last = (oc == DC - 1 and r == 3)
                            P.op("te", "matmul",
                                 OUT[oc // 2][:, (oc % 2) * 256:(oc % 2) * 256 + 256],
                                 ev2zb[0:4, r * 1024 + oc * 128:
                                       r * 1024 + oc * 128 + 128],
                                 t2_bf[0:4, r * 256:(r + 1) * 256],
                                 start=False,
                                 stop=(r == 3 and oc % 2 == 1), inc=last)
                    commit(P, "te", f"corr{s}", reads=[R["t2_bf"]],
                           writes=[res("OUT", i) for i in range(4)])

                    acquire(P, "ve",
                            reads=[res("rtb", q)] + [res("OUT", i) for i in range(4)],
                            writes=[res("outb", q)])
                    P.wait_ck("ve", f"corr{s}")
                    for oc in range(DC):
                        P.op("ve", "scalar_tensor_tensor",
                             outb[q][:, oc * 256:(oc + 1) * 256],
                             OUT[oc // 2][:, (oc % 2) * 256:(oc % 2) * 256 + 256],
                             b2_sb[:, oc:oc + 1],
                             rtb[q][:, oc * 256:(oc + 1) * 256],
                             ALU.add, ALU.add, inc=(oc == DC - 1))
                    commit(P, "ve", f"oute{s}",
                           reads=[res("rtb", q)] + [res("OUT", i) for i in range(4)],
                           writes=[res("outb", q)])

                    acquire(P, "gp", reads=[res("outb", q)])
                    P.dma("gp", f"sso{q}",
                          outd.ap().rearrange("(a b) c -> b a c", b=128)
                          [:, :, t0:t0 + TB],
                          bass.AP(outb[q], 0, [[2048, 128], [256, 8], [1, 256]]))
                    commit_dma(P, f"sso{q}", f"os{s}", reads=[res("outb", q)])

                P.wait_dma("gp", "sso0", P.dcnt["sso0"])
                P.wait_dma("gp", "sso1", P.dcnt["sso1"])

            Pc2 = Prog("count", ck)
            Pc2.cnt = dict(totals_A[0])
            Pc2.dcnt = dict(totals_A[1])
            Pc2.shadow = {e: {tn: dict(d) for tn, d in sh.items()}
                          for e, sh in track_A[0].items()}
            Pc2.self_hw = dict(track_A[1])
            emit_B(Pc2)
            totals_B = (dict(Pc2.cnt), dict(Pc2.dcnt))

            def mkprog(eng, e):
                Px = Prog("emit", ck, eng, {eng: e}, esem, dsem)
                Px.cnt = dict(totals_A[0])
                Px.dcnt = dict(totals_A[1])
                Px.shadow = {e2: {tn: dict(d) for tn, d in sh.items()}
                             for e2, sh in track_A[0].items()}
                Px.self_hw = dict(track_A[1])
                return Px

            with nc.Block() as blk2:
                @blk2.tensor
                def _(e):
                    Px = mkprog("te", e)
                    emit_B(Px)
                    assert Px.cnt == totals_B[0]

                @blk2.vector
                def _(e):
                    emit_B(mkprog("ve", e))

                @blk2.scalar
                def _(e):
                    emit_B(mkprog("se", e))

                @blk2.sync
                def _(e):
                    emit_B(mkprog("sy", e))

                @blk2.gpsimd
                def _(e):
                    Px = mkprog("gp", e)
                    emit_B(Px)
                    assert Px.dcnt == totals_B[1]

    return nc


def _masks():
    i = np.arange(128)[:, None]
    j = np.arange(256)[None, :]
    base = (j > i) & (j <= i + 128)
    first = base & (j >= 128)
    mb = np.where(base, 0.0, -1e30).astype(np.float32)
    mf = np.where(first, 0.0, -1e30).astype(np.float32)
    return mb, mf


def kernel(L0_attn_x, L0_attn_g, L0_attn_Wqkv, L0_attn_Wo,
           L0_ffn_W1, L0_ffn_b1, L0_ffn_W2, L0_ffn_b2, L0_ffn_scale):
    X = np.ascontiguousarray(np.asarray(L0_attn_x, dtype=np.float32))
    g = np.asarray(L0_attn_g, dtype=np.float32)
    Wqkv = np.ascontiguousarray(np.asarray(L0_attn_Wqkv, dtype=np.float32))
    Wo = np.ascontiguousarray(np.asarray(L0_attn_Wo, dtype=np.float32))
    W1t = np.ascontiguousarray(np.asarray(L0_ffn_W1, dtype=np.float32).T)
    W2t = np.ascontiguousarray(np.asarray(L0_ffn_W2, dtype=np.float32).T)
    b1 = np.asarray(L0_ffn_b1, dtype=np.float32)
    b2 = np.asarray(L0_ffn_b2, dtype=np.float32)
    sc = np.asarray(L0_ffn_scale, dtype=np.float32).reshape(1, 1)

    gm = np.ascontiguousarray(g.reshape(8, 128).T)
    b1m = np.ascontiguousarray(b1.reshape(32, 128).T)
    b2m = np.ascontiguousarray(b2.reshape(8, 128).T)
    ident = np.eye(128, dtype=np.float32)
    mb, mf = _masks()
    mb = np.ascontiguousarray(np.hstack([mb, mb]))
    mf = np.ascontiguousarray(np.hstack([mf, mf]))

    nc = build_nc()
    in_maps = []
    for c in range(8):
        b = c // 4
        t0 = (c % 4) * TC
        xh = (np.zeros((WIN, D), np.float32) if t0 == 0
              else np.ascontiguousarray(X[b, t0 - WIN:t0]))
        in_maps.append({
            "x": np.ascontiguousarray(X[b, t0:t0 + TC]),
            "xh": xh,
            "xle": np.ascontiguousarray(X[b, 1:17:2]),
            "xlo": np.ascontiguousarray(X[b, 2:17:2]),
            "wqkv": Wqkv, "wo": Wo, "w1t": W1t, "w2t": W2t,
            "gm": gm, "b1m": b1m, "b2m": b2m, "scv": sc,
            "mask0": (mf if t0 == 0 else mb), "maskr": mb,
            "ident": ident,
        })

    res = run_bass_kernel_spmd(nc, in_maps, core_ids=list(range(8)))
    out = np.empty((2, 4 * TC, D), np.float32)
    for c in range(8):
        b = c // 4
        t0 = (c % 4) * TC
        out[b, t0:t0 + TC, :] = res.results[c]["out"].T
    return out


# version 34
# speedup vs baseline: 1.0144x; 1.0144x over previous
import numpy as np
import concourse.bass as bass
import concourse.mybir as mybir
from concourse.bass_utils import run_bass_kernel_spmd

F32 = mybir.dt.float32
BF16 = mybir.dt.bfloat16
AF = mybir.ActivationFunctionType
ALU = mybir.AluOpType
AX = mybir.AxisListType.X

D = 1024
DC = 8
NH = 16
DH = 64
WIN = 128
TC = 4096
NBLK = TC // WIN
HF = 4096
HCN = 32
TB = 256
NST = TC // TB

ENGS = ("te", "ve", "se", "sy", "gp")
INTERLEAVE = True
INTERLEAVE_K = 16
DSEMS = ("sx0", "sx1", "swa", "swg", "slt", "sst0", "sst1",
         "swb1", "swb2", "sbi0", "sbi1", "sso0", "sso1") + tuple(
    f"swA{i}" for i in range(8)) + tuple(f"swB{i}" for i in range(8))


class Res:
    __slots__ = ("w", "r")

    def __init__(self):
        self.w = None
        self.r = []


def _rect(x):
    if not isinstance(x, bass.AP):
        return None
    t = x.tensor
    ncol = int(t.shape[-1])
    off = int(x.offset)
    p_ext = 0
    c_ext = 0
    for pr in x.ap:
        stride, n = int(pr[0]), int(pr[1])
        if n <= 1:
            continue
        if stride >= ncol and stride % ncol == 0:
            p_ext += (stride // ncol) * (n - 1)
        else:
            c_ext += stride * (n - 1)
    p0 = off // ncol
    c0 = off % ncol
    return (t.name, p0, p0 + p_ext + 1, c0, c0 + c_ext + 1)


_TRACKED = ("ve", "se")


class Prog:
    def __init__(self, mode, ck, filt=None, engs=None, esem=None, dsem=None):
        self.mode = mode
        self.ck = ck
        self.filt = filt
        self.engs = engs or {}
        self.esem = esem or {}
        self.dsem = dsem or {}
        self.cnt = {e: 0 for e in ENGS}
        self.dcnt = {s: 0 for s in DSEMS}
        self.shadow = {e: {} for e in _TRACKED}
        self.self_hw = {e: 0 for e in _TRACKED}

    def active(self, eng):
        return self.mode == "emit" and self.filt == eng

    def wait_ck(self, eng, name):
        if name is None:
            return
        kind, key, val = self.ck[name]
        if kind == "eng":
            if key == eng or val == 0:
                return
            if self.active(eng):
                self.engs[eng].wait_ge(self.esem[key], val)
        else:
            if val == 0:
                return
            if self.active(eng):
                self.engs[eng].wait_ge(self.dsem[key], val)

    def wait_dma(self, eng, dsname, val):
        if val and self.active(eng):
            self.engs[eng].wait_ge(self.dsem[dsname], val)

    def op(self, eng, meth, *a, inc=False, **k):
        if eng in _TRACKED:
            wr, rd = [], []
            for i, x in enumerate(a):
                r = _rect(x)
                if r is not None:
                    (wr if i == 0 else rd).append(r)
            for key in ("bias", "scale"):
                r = _rect(k.get(key))
                if r is not None:
                    rd.append(r)
            r = _rect(k.get("accum_out"))
            if r is not None:
                wr.append(r)
            need = 0
            sh = self.shadow[eng]
            for (tn, p0, p1, c0, c1) in rd:
                for (isw, q0, q1, d0, d1), val in sh.get(tn, {}).items():
                    if (isw and val > need and p0 < q1 and q0 < p1
                            and c0 < d1 and d0 < c1):
                        need = val
            for (tn, p0, p1, c0, c1) in wr:
                for (isw, q0, q1, d0, d1), val in sh.get(tn, {}).items():
                    if (val > need and p0 < q1 and q0 < p1
                            and c0 < d1 and d0 < c1):
                        need = val
            if need > self.self_hw[eng]:
                self.self_hw[eng] = need
                if self.active(eng):
                    self.engs[eng].wait_ge(self.esem[eng], need)
            self.cnt[eng] += 1
            v = self.cnt[eng]
            for (tn, p0, p1, c0, c1) in wr:
                d = sh.setdefault(tn, {})
                d[(True, p0, p1, c0, c1)] = v
                if len(d) > 96:
                    hw = self.self_hw[eng]
                    for kk in [kk for kk, vv in d.items() if vv <= hw]:
                        del d[kk]
            for (tn, p0, p1, c0, c1) in rd:
                d = sh.setdefault(tn, {})
                d[(False, p0, p1, c0, c1)] = v
                if len(d) > 96:
                    hw = self.self_hw[eng]
                    for kk in [kk for kk, vv in d.items() if vv <= hw]:
                        del d[kk]
            ins = None
            if self.active(eng):
                ins = getattr(self.engs[eng], meth)(*a, **k)
                ins.then_inc(self.esem[eng], 1)
            return ins
        ins = None
        if self.active(eng):
            ins = getattr(self.engs[eng], meth)(*a, **k)
        if inc:
            self.cnt[eng] += 1
            if ins is not None:
                ins.then_inc(self.esem[eng], 1)
        return ins

    def dma(self, eng, dsname, out, in_, **kw):
        self.dcnt[dsname] += 16
        if self.active(eng):
            self.engs[eng].dma_start(out=out, in_=in_, **kw).then_inc(
                self.dsem[dsname], 16)

    def ckpt(self, eng, name):
        if self.mode == "count":
            assert name not in self.ck, name
            self.ck[name] = ("eng", eng, self.cnt[eng])

    def dma_ckpt(self, dsname, name):
        if self.mode == "count":
            assert name not in self.ck, name
            self.ck[name] = ("dma", dsname, self.dcnt[dsname])


def acquire(P, eng, reads=(), writes=()):
    names = []
    for x in reads:
        names.append(x.w)
    for x in writes:
        names.append(x.w)
        names.extend(x.r)
    for n in dict.fromkeys(names):
        P.wait_ck(eng, n)


def commit(P, eng, name, reads=(), writes=()):
    P.ckpt(eng, name)
    for x in reads:
        x.r.append(name)
    for x in writes:
        x.w = name
        x.r = []


def commit_dma(P, dsname, name, reads=(), writes=()):
    P.dma_ckpt(dsname, name)
    for x in reads:
        x.r.append(name)
    for x in writes:
        x.w = name
        x.r = []


def build_nc(debug=False):
    nc = bass.Bass("TRN2", target_bir_lowering=False)
    scr_kind = {"kind": "ExternalOutput"} if debug else {}

    xd = nc.dram_tensor("x", [TC, D], F32, kind="ExternalInput")
    xhd = nc.dram_tensor("xh", [WIN, D], F32, kind="ExternalInput")
    xled = nc.dram_tensor("xle", [8, D], F32, kind="ExternalInput")
    xlod = nc.dram_tensor("xlo", [8, D], F32, kind="ExternalInput")
    wqkvd = nc.dram_tensor("wqkv", [D, 3 * D], F32, kind="ExternalInput")
    wod = nc.dram_tensor("wo", [D, D], F32, kind="ExternalInput")
    w1td = nc.dram_tensor("w1t", [D, HF], F32, kind="ExternalInput")
    w2td = nc.dram_tensor("w2t", [HF, D], F32, kind="ExternalInput")
    gmd = nc.dram_tensor("gm", [128, 8], F32, kind="ExternalInput")
    b1md = nc.dram_tensor("b1m", [128, 32], F32, kind="ExternalInput")
    b2md = nc.dram_tensor("b2m", [128, 8], F32, kind="ExternalInput")
    scd = nc.dram_tensor("scv", [1, 1], F32, kind="ExternalInput")
    m0d = nc.dram_tensor("mask0", [128, 512], F32, kind="ExternalInput")
    mrd = nc.dram_tensor("maskr", [128, 512], F32, kind="ExternalInput")
    identd = nc.dram_tensor("ident", [128, 128], F32, kind="ExternalInput")
    scr_r = nc.dram_tensor("scr_r", [D, TC], F32, **scr_kind)
    dbgst = nc.dram_tensor("dbgst", [128, 80], F32, **scr_kind) if debug else None
    scr_xn = nc.dram_tensor("scr_xn", [D, TC], BF16, **scr_kind)
    outd = nc.dram_tensor("out", [D, TC], F32, kind="ExternalOutput")

    ck = {}

    import contextlib
    with contextlib.ExitStack() as stack:
        def sb(name, shape, dtype):
            return stack.enter_context(nc.sbuf_tensor(name, shape, dtype))

        def sem(name):
            return stack.enter_context(nc.semaphore(name))

        gm_sb = sb("gm_sb", [128, 8], F32)
        b1_sb = sb("b1_sb", [128, 32], F32)
        b2_sb = sb("b2_sb", [128, 8], F32)
        sc8 = sb("sc8", [8, 1], F32)
        idf = sb("idf", [128, 128], F32)
        idb = sb("idb", [128, 128], BF16)
        m0b = sb("m0b", [128, 512], BF16)
        mrb = sb("mrb", [128, 512], BF16)
        ev1zb = sb("ev1zb", [4, HF], BF16)
        ev2zb = sb("ev2zb", [4, HF], BF16)
        oddtb = sb("oddtb", [128, 64], BF16)
        stats = sb("stats", [128, 80], F32)
        negmx = sb("negmx", [128, 4], F32)
        ssum = sb("ssum", [128, 4], F32)
        srec = sb("srec", [128, 4], F32)
        s_te = sem("s_te")
        s_ve = sem("s_ve")
        s_se = sem("s_se")
        dsh = {n: sem("d_" + n) for n in DSEMS}

        esem = {"te": s_te, "ve": s_ve, "se": s_se}
        dsem = dsh

        with contextlib.ExitStack() as stA:
            def sa(name, shape, dtype):
                return stA.enter_context(nc.sbuf_tensor(name, shape, dtype))

            def pa(name, shape, dtype):
                return stA.enter_context(nc.psum_tensor(name, shape, dtype))

            wqkv_b = sa("wqkv_b", [128, DC * 3 * D], BF16)
            wo_b = sa("wo_b", [128, DC * D], BF16)

            xbuf0 = sa("xbuf0", [128, D], F32)
            xbuf1 = sa("xbuf1", [128, D], F32)
            xc0 = sa("xc0", [128, D], F32)
            xc1 = sa("xc1", [128, D], F32)
            xn2b0 = sa("xn2b0", [128, D], BF16)
            xn2b1 = sa("xn2b1", [128, D], BF16)
            xnb0 = sa("xnb0", [128, D], BF16)
            xnb1 = sa("xnb1", [128, D], BF16)
            r320 = sa("r320", [128, D], F32)
            r321 = sa("r321", [128, D], F32)
            xn2tP = sa("xn2tP", [128, 2048], BF16)
            xntP0 = sa("xntP0", [128, 2048], BF16)
            xntP1 = sa("xntP1", [128, 2048], BF16)
            xntP = (xntP0, xntP1)
            rtP0 = sa("rtP0", [128, 2048], F32)
            rtP1 = sa("rtP1", [128, 2048], F32)
            rtP = (rtP0, rtP1)
            qtP0 = sa("qtP0", [128, 2048], BF16)
            qtP1 = sa("qtP1", [128, 2048], BF16)
            kvKP0 = sa("kvKP0", [128, 3072], BF16)
            kvKP1 = sa("kvKP1", [128, 3072], BF16)
            kvVP0 = sa("kvVP0", [128, 2048], BF16)
            kvVP1 = sa("kvVP1", [128, 2048], BF16)
            ebP0 = sa("ebP0", [128, 512], BF16)
            ebP1 = sa("ebP1", [128, 512], BF16)
            ptbP0 = sa("ptbP0", [128, 512], BF16)
            ptbP1 = sa("ptbP1", [128, 512], BF16)
            dgb0 = sa("dgb0", [128, 128], BF16)
            dgb1 = sa("dgb1", [128, 128], BF16)
            dgb2 = sa("dgb2", [128, 128], BF16)
            dgb3 = sa("dgb3", [128, 128], BF16)
            otP0 = sa("otP0", [128, 2048], BF16)
            otP1 = sa("otP1", [128, 2048], BF16)
            routP0 = sa("routP0", [128, 2048], F32)
            routP1 = sa("routP1", [128, 2048], F32)
            routP = (routP0, routP1)
            PS0 = pa("PS0", [128, 512], F32)
            PS1 = pa("PS1", [128, 512], F32)
            PP0 = pa("PP0", [128, 512], F32)
            PP1 = pa("PP1", [128, 512], F32)
            PO0 = pa("PO0", [128, 512], F32)
            PO1 = pa("PO1", [128, 512], F32)
            PTb0 = pa("PTb0", [128, 1024], BF16)
            PV0 = pa("PV0", [128, 512], F32)
            PV1 = PV0

            xbuf = (xbuf0, xbuf1)
            xcb = (xc0, xc1)
            xn2b = (xn2b0, xn2b1)
            xnb = (xnb0, xnb1)
            r32 = (r320, r321)
            qtP = (qtP0, qtP1)
            kvKP = (kvKP0, kvKP1)
            kvVP = (kvVP0, kvVP1)
            otP = (otP0, otP1)
            ebP = (ebP0, ebP1)
            ptbP = (ptbP0, ptbP1)
            dgb = (dgb0, dgb1, dgb2, dgb3)

            PS = (PS0, PS1)
            PP = (PP0, PP1)
            PO = (PO0, PO1)
            PV = (PV0, PV1)

            def scol(slot, c, rows=128):
                return stats[0:rows, slot * 16 + c: slot * 16 + c + 1]

            def emit_A(P):
                R = {n: Res() for n in (
                    "xbuf0", "xbuf1", "xc0", "xc1", "xn2b0", "xn2b1",
                    "xnb0", "xnb1", "r320", "r321", "xn2tP",
                    "xntP0", "xntP1", "rtP0", "rtP1", "qtP0", "qtP1",
                    "kvKP0", "kvKP1", "kvVP0", "kvVP1",
                    "ebP0", "ebP1", "ptbP0", "ptbP1", "PV0", "PV1",
                    "dgb0", "dgb1", "dgb2", "dgb3", "otP0", "otP1",
                    "routP0", "routP1", "PS0", "PS1", "PP0", "PP1",
                    "PO0", "PO1", "PTb0", "wqkv_b",
                    "stats0", "stats1", "stats2", "stats3", "stats4",
                    "negmx0", "negmx1", "negmx2", "negmx3",
                    "ssum0", "ssum1", "ssum2", "ssum3",
                    "srec0", "srec1", "srec2", "srec3",
                    "ev1zb", "ev2zb", "oddtb",
                )}

                R["PV1"] = R["PV0"]

                def res(base, i=None):
                    return R[base + ("" if i is None else str(i))]

                P.dma("sy", "swa", gm_sb[:], gmd.ap())
                P.dma("sy", "swa", b1_sb[:], b1md.ap())
                P.dma("sy", "swa", b2_sb[:], b2md.ap())
                P.dma("sy", "swa", idf[:], identd.ap())
                P.dma("sy", "swa", xbuf0[0:8, :], xled.ap())
                P.dma("sy", "swa", xbuf1[0:8, :], xlod.ap())
                P.dma("sy", "swa", r320[:], xhd.ap())
                commit_dma(P, "swa", "pload",
                           writes=[R["xbuf0"], R["xbuf1"], R["r320"]])

                P.dma("gp", "swg", idb[:], identd.ap())
                P.dma("gp", "swg", m0b[:], m0d.ap())
                P.dma("gp", "swg", mrb[:], mrd.ap())
                P.dma("gp", "swg", wo_b[:],
                      wod.ap().rearrange("(a b) c -> b a c", b=128))
                P.dma("gp", "swg", sc8[:], bass.AP(scd, 0, [[0, 8], [1, 1]]))
                commit_dma(P, "swg", "pcast", writes=[])

                for dcc in range(DC):
                    for hf_ in range(2):
                        acquire(P, "sy", writes=[R["rtP0"]])
                        P.dma("sy", "sbi0", rtP0[:, 0:1536],
                              wqkvd.ap()[dcc * 128:(dcc + 1) * 128,
                                         hf_ * 1536:(hf_ + 1) * 1536])
                        commit_dma(P, "sbi0", f"wql{dcc}_{hf_}",
                                   writes=[R["rtP0"]])
                        acquire(P, "ve", reads=[R["rtP0"]], writes=[R["wqkv_b"]])
                        P.wait_ck("ve", "pload")
                        if hf_ == 0:
                            P.op("ve", "tensor_scalar",
                                 wqkv_b[:, dcc * 3072: dcc * 3072 + 1024],
                                 rtP0[:, 0:1024], gm_sb[:, dcc:dcc + 1], 0.125,
                                 ALU.mult, ALU.mult)
                            P.op("ve", "tensor_scalar",
                                 wqkv_b[:, dcc * 3072 + 1024:
                                        dcc * 3072 + 1536],
                                 rtP0[:, 1024:1536], gm_sb[:, dcc:dcc + 1],
                                 None, ALU.mult, inc=True)
                        else:
                            P.op("ve", "tensor_scalar",
                                 wqkv_b[:, dcc * 3072 + 1536:
                                        dcc * 3072 + 3072],
                                 rtP0[:, 0:1536], gm_sb[:, dcc:dcc + 1],
                                 None, ALU.mult, inc=True)
                        commit(P, "ve", f"wqc{dcc}_{hf_}", reads=[R["rtP0"]],
                               writes=[R["wqkv_b"]])

                for li, (src, dst, xres, dres) in enumerate(
                        ((xbuf0, xc0, R["xbuf0"], R["xc0"]),
                         (xbuf1, xc1, R["xbuf1"], R["xc1"]))):
                    sl = 3
                    cb = 32 * li
                    def lc(c):
                        return stats[0:8, 48 + li * 16 + c: 48 + li * 16 + c + 1]
                    sres = R["stats" + str(3 + li)]
                    sqi = (xnb0, xnb1)[li]
                    sqs = (xn2b0, xn2b1)[li]
                    sqir = res("xnb", li)
                    sqsr = res("xn2b", li)
                    acquire(P, "se", reads=[xres], writes=[sqir, sqsr, sres])
                    P.op("se", "activation", sqi[0:8, :], src[0:8, :],
                         AF.Identity, accum_out=lc(0))
                    P.op("se", "activation", sqs[0:8, :], src[0:8, :],
                         AF.Square, accum_out=lc(3), inc=True)
                    commit(P, "se", f"lsq{li}", reads=[xres],
                           writes=[sqir, sqsr])

                    acquire(P, "ve", writes=[sres])
                    P.wait_ck("ve", f"lsq{li}")
                    P.op("ve", "tensor_scalar", lc(1), lc(0), 1.0 / D, None, ALU.mult)
                    P.op("ve", "tensor_scalar", lc(4), lc(3), 1.0 / D, None, ALU.mult)
                    P.op("ve", "tensor_scalar", lc(15), lc(0), lc(0),
                         1.0 / (D * D), ALU.mult, ALU.mult)
                    P.op("ve", "tensor_scalar", lc(2), lc(0), -1.0 / D, None,
                         ALU.mult)
                    P.op("ve", "tensor_tensor", lc(5), lc(4), lc(15), ALU.subtract)
                    P.op("ve", "tensor_scalar", lc(14), lc(5), 1e-5, None, ALU.add,
                         inc=True)
                    commit(P, "ve", f"lst{li}", writes=[sres])

                    P.wait_ck("se", f"lst{li}")
                    P.op("se", "activation", lc(12), lc(14), AF.Ln)
                    P.op("se", "activation", lc(6), lc(12), AF.Exp, scale=-0.5,
                         inc=True)
                    commit(P, "se", f"lrs{li}", writes=[sres])
                    P.wait_ck("ve", f"lrs{li}")
                    P.op("ve", "tensor_tensor", lc(15), lc(6), lc(6), ALU.mult)
                    P.op("ve", "tensor_tensor", lc(13), lc(15), lc(14), ALU.mult)
                    P.op("ve", "tensor_scalar", lc(12), lc(13), -0.5, 1.5,
                         ALU.mult, ALU.add)
                    P.op("ve", "tensor_tensor", lc(7), lc(6), lc(12), ALU.mult)
                    P.op("ve", "tensor_tensor", lc(8), lc(2), lc(7), ALU.mult)
                    commit(P, "ve", f"lnm{li}", writes=[sres])
                    acquire(P, "ve", reads=[xres], writes=[dres])
                    P.op("ve", "tensor_scalar", dst[0:8, :], src[0:8, :],
                         lc(7), lc(8), ALU.mult, ALU.add, inc=True)
                    commit(P, "ve", f"lxn{li}", reads=[xres, sres],
                           writes=[dres])

                acquire(P, "ve", writes=[R["ev1zb"]])
                P.op("ve", "memset", ev1zb[:], 0.0, inc=True)
                commit(P, "ve", "evz1", writes=[R["ev1zb"]])
                acquire(P, "gp", reads=[R["xc0"]], writes=[R["ev1zb"]])
                P.wait_ck("gp", "lxn0")
                for i in range(4):
                    P.dma("gp", "slt", ev1zb[i:i + 1, i * 1024:(i + 1) * 1024],
                          xc0.ap()[i:i + 1, 0:1024])

                acquire(P, "ve", writes=[R["ev2zb"]])
                P.op("ve", "memset", ev2zb[:], 0.0, inc=True)
                commit(P, "ve", "evz2", writes=[R["ev2zb"]])
                acquire(P, "gp", reads=[R["xc0"]], writes=[R["ev2zb"]])
                for i in range(4):
                    for r in range(4):
                        P.dma("gp", "slt",
                              ev2zb[i:i + 1,
                                    r * 1024 + i * 256: r * 1024 + i * 256 + 256],
                              xc0.ap()[4 + i:5 + i, r:1024:4],
                              allow_slow_non_contiguous=True)
                commit_dma(P, "slt", "evd2",
                           reads=[R["xc0"]],
                           writes=[R["ev1zb"], R["ev2zb"]])

                acquire(P, "te", reads=[R["xc1"]], writes=[R["PS0"]])
                P.wait_ck("te", "lxn1")
                P.wait_ck("te", "pload")
                for dcc in range(DC):
                    P.op("te", "transpose",
                         PS0[:, dcc * 8: dcc * 8 + 8],
                         xc1[0:8, dcc * 128:(dcc + 1) * 128],
                         idf[0:8, 0:8], inc=(dcc == DC - 1))
                commit(P, "te", "oddt", reads=[R["xc1"]], writes=[R["PS0"]])
                acquire(P, "ve", reads=[R["PS0"]], writes=[R["oddtb"]])
                P.wait_ck("ve", "oddt")
                P.op("ve", "tensor_copy", oddtb[:], PS0[:, 0:64], inc=True)
                commit(P, "ve", "oddte", reads=[R["PS0"]], writes=[R["oddtb"]])

                def emit_prep(b):
                    for _ in emit_prep_gen(b):
                        pass

                def emit_prep_gen(b):
                    halo = b < 0
                    p = b % 2 if not halo else 1
                    slot = 2 if halo else p
                    xs = r320 if halo else xbuf[p]
                    xsr = R["r320"] if halo else res("xbuf", p)
                    xcs = xc1 if halo else xcb[p]
                    xcr = R["xc1"] if halo else res("xc", p)
                    x2s = xn2b1 if halo else xn2b[p]
                    x2r = R["xn2b1"] if halo else res("xn2b", p)
                    tag = "h" if halo else str(b)

                    def sc_(c):
                        return scol(slot, c)

                    if not halo:
                        acquire(P, "sy", writes=[xsr])
                        P.dma("sy", f"sx{p}", xs[:],
                              xd.ap()[b * 128:(b + 1) * 128, :])
                        commit_dma(P, f"sx{p}", f"ldx{b}", writes=[xsr])
                    yield None

                    sqi = xnb[1] if halo else xnb[p]
                    sqir = res("xnb", 1 if halo else p)
                    acquire(P, "se", reads=[xsr],
                            writes=[res("stats", slot), sqir, x2r])
                    P.op("se", "activation", sqi[:], xs[:],
                         AF.Identity, accum_out=sc_(0))
                    P.op("se", "activation", x2s[:], xs[:], AF.Square,
                         accum_out=sc_(3), inc=True)
                    commit(P, "se", f"sq{tag}", reads=[xsr],
                           writes=[sqir, x2r])
                    yield None

                    acquire(P, "ve", writes=[res("stats", slot)])
                    P.wait_ck("ve", f"sq{tag}")
                    P.op("ve", "tensor_scalar", sc_(1), sc_(0), 1.0 / D, None, ALU.mult)
                    P.op("ve", "tensor_scalar", sc_(4), sc_(3), 1.0 / D, None, ALU.mult)
                    P.op("ve", "tensor_scalar", sc_(15), sc_(0), sc_(0),
                         1.0 / (D * D), ALU.mult, ALU.mult)
                    P.op("ve", "tensor_scalar", sc_(2), sc_(0), -1.0 / D, None,
                         ALU.mult)
                    P.op("ve", "tensor_tensor", sc_(5), sc_(4), sc_(15),
                         ALU.subtract)
                    P.op("ve", "tensor_scalar", sc_(14), sc_(5), 1e-5, None,
                         ALU.add, inc=True)
                    commit(P, "ve", f"st{tag}",
                           writes=[res("stats", slot)])
                    yield None

                    P.wait_ck("se", f"st{tag}")
                    P.op("se", "activation", sc_(12), sc_(14), AF.Ln)
                    P.op("se", "activation", sc_(6), sc_(12), AF.Exp, scale=-0.5,
                         inc=True)
                    commit(P, "se", f"ch_{tag}", writes=[res("stats", slot)])
                    yield None

                    acquire(P, "ve", reads=[xsr], writes=[xcr])
                    P.wait_ck("ve", f"ch_{tag}")
                    P.op("ve", "tensor_tensor", sc_(15), sc_(6), sc_(6), ALU.mult)
                    P.op("ve", "tensor_tensor", sc_(13), sc_(15), sc_(14), ALU.mult)
                    P.op("ve", "tensor_scalar", sc_(12), sc_(13), -0.5, 1.5,
                         ALU.mult, ALU.add)
                    P.op("ve", "tensor_tensor", sc_(7), sc_(6), sc_(12), ALU.mult)
                    P.op("ve", "tensor_tensor", sc_(3), sc_(7), sc_(7), ALU.mult)
                    P.op("ve", "tensor_scalar", sc_(11), sc_(7), 1.0, None,
                         ALU.add)
                    P.op("ve", "tensor_scalar", sc_(8), sc_(5), sc_(3), 1e-6,
                         ALU.mult, ALU.add)
                    P.op("ve", "tensor_scalar", xcs[:], xs[:], sc_(2), None,
                         ALU.add, inc=True)
                    commit(P, "ve", f"ch2_{tag}", reads=[xsr],
                           writes=[res("stats", slot), xcr])
                    yield None

                    P.wait_ck("se", f"ch2_{tag}")
                    P.op("se", "activation", sc_(12), sc_(8), AF.Ln)
                    P.op("se", "activation", sc_(9), sc_(12), AF.Exp, scale=-0.5,
                         inc=True)
                    commit(P, "se", f"ch3_{tag}", writes=[res("stats", slot)])
                    yield None

                    acquire(P, "ve", reads=[xcr], writes=[x2r])
                    P.wait_ck("ve", f"ch3_{tag}")
                    P.op("ve", "tensor_tensor", sc_(0), sc_(9), sc_(9), ALU.mult)
                    P.op("ve", "tensor_tensor", sc_(13), sc_(0), sc_(8), ALU.mult)
                    P.op("ve", "tensor_scalar", sc_(4), sc_(13), -0.5, 1.5,
                         ALU.mult, ALU.add)
                    P.op("ve", "tensor_tensor", sc_(2), sc_(9), sc_(4), ALU.mult)
                    P.op("ve", "tensor_tensor", sc_(10), sc_(7), sc_(2), ALU.mult)
                    P.op("ve", "tensor_scalar", x2s[:], xcs[:], sc_(10), None,
                         ALU.mult, inc=True)
                    commit(P, "ve", f"x2{tag}",
                           reads=[xcr, res("stats", slot)], writes=[x2r])
                    yield None

                    if not halo:
                        acquire(P, "ve", reads=[xcr],
                                writes=[res("xnb", p), res("r32", p)])
                        P.op("ve", "tensor_scalar", xnb[p][:], xcs[:], sc_(7),
                             None, ALU.mult)
                        P.op("ve", "tensor_scalar", r32[p][:], xcs[:], sc_(11),
                             sc_(1), ALU.mult, ALU.add, inc=True)
                        commit(P, "ve", f"xnr{tag}",
                               reads=[xcr, res("stats", slot)],
                               writes=[res("xnb", p), res("r32", p)])


                NPAIR = NBLK // 2

                def emit_tp(tb, bank, bankres, src_t, srcres, dst_t, dstres,
                            off, ident, tag):
                    acquire(P, "te", reads=[srcres], writes=[bankres])
                    P.wait_ck("te", "pcast")
                    for dcc in range(DC):
                        P.op("te", "transpose",
                             bank[:, dcc * 128:(dcc + 1) * 128],
                             src_t[:, dcc * 128:(dcc + 1) * 128], ident,
                             inc=(dcc == DC - 1))
                    commit(P, "te", f"tp{tag}", reads=[srcres],
                           writes=[bankres])
                    acquire(P, "ve", reads=[bankres], writes=[dstres])
                    P.wait_ck("ve", f"tp{tag}")
                    P.op("ve", "tensor_copy",
                         bass.AP(dst_t, off, [[2048, 128], [256, 8], [1, 128]]),
                         bass.AP(bank, 0, [[1024, 128], [128, 8], [1, 128]]),
                         inc=True)
                    commit(P, "ve", f"tpe{tag}", reads=[bankres],
                           writes=[dstres])

                def emit_rt(b, half, tag):
                    p = b % 2
                    qj = (b // 2) % 2
                    bres = res("PO", half)
                    acquire(P, "te", reads=[res("r32", p)], writes=[bres])
                    for di in range(4):
                        dcc = half * 4 + di
                        P.op("te", "transpose",
                             PO[half][:, di * 128:(di + 1) * 128],
                             r32[p][:, dcc * 128:(dcc + 1) * 128], idf[:],
                             inc=(di == 3))
                    commit(P, "te", f"trt{tag}", reads=[res("r32", p)],
                           writes=[bres])
                    acquire(P, "ve", reads=[bres], writes=[res("rtP", qj)])
                    P.wait_ck("ve", f"trt{tag}")
                    P.op("ve", "tensor_copy",
                         bass.AP(rtP[qj], half * 4 * 256 + p * 128,
                                 [[2048, 128], [256, 4], [1, 128]]),
                         bass.AP(PO[half], 0, [[512, 128], [128, 4], [1, 128]]),
                         inc=True)
                    commit(P, "ve", f"ert{tag}", reads=[bres],
                           writes=[res("rtP", qj)])

                def emit_pre_pair(j):
                    q = j % 2
                    B0, B1 = 2 * j, 2 * j + 1
                    emit_tp(j, PTb0, R["PTb0"], xn2b0, R["xn2b0"],
                            xn2tP, R["xn2tP"], 0, idb[:], f"x2_{B0}")
                    yield None
                    emit_tp(j, PTb0, R["PTb0"], xn2b1, R["xn2b1"],
                            xn2tP, R["xn2tP"], 128, idb[:], f"x2_{B1}")
                    yield None
                    for L in range(8):
                        m0 = 2 * L
                        bank = L % 2
                        bres = res("PO", bank)
                        acquire(P, "te", reads=[R["xn2tP"], R["wqkv_b"]],
                                writes=[bres])
                        for mi in range(2):
                            m = m0 + mi
                            for dcc in range(DC):
                                last = (mi == 1 and dcc == DC - 1)
                                P.op("te", "matmul",
                                     PO[bank][:, mi * 256:(mi + 1) * 256],
                                     wqkv_b[:, dcc * 3072 + m * 128:
                                            dcc * 3072 + m * 128 + 128],
                                     xn2tP[:, dcc * 256:(dcc + 1) * 256],
                                     start=(mi == 0 and dcc == 0),
                                     stop=last, inc=last)
                        commit(P, "te", f"qk{j}_{L}", reads=[R["xn2tP"]],
                               writes=[bres])
                        if L < 4:
                            tres = res("qtP", q)
                            acquire(P, "ve", reads=[bres], writes=[tres])
                            P.wait_ck("ve", f"qk{j}_{L}")
                            P.op("ve", "tensor_copy",
                                 qtP[q][:, L * 512:(L + 1) * 512],
                                 PO[bank][:, 0:512], inc=True)
                            commit(P, "ve", f"eqk{j}_{L}", reads=[bres],
                                   writes=[tres])
                        else:
                            tres = res("kvKP", q)
                            acquire(P, "ve", reads=[bres], writes=[tres])
                            P.wait_ck("ve", f"qk{j}_{L}")
                            P.op("ve", "tensor_copy",
                                 bass.AP(kvKP[q], (L - 4) * 2 * 384 + 128,
                                         [[3072, 128], [384, 2], [1, 256]]),
                                 bass.AP(PO[bank], 0,
                                         [[512, 128], [256, 2], [1, 256]]),
                                 inc=True)
                            commit(P, "ve", f"eqk{j}_{L}", reads=[bres],
                                   writes=[tres])
                        yield None
                    for blk in range(2):
                        for rnd in range(2):
                            bank = rnd
                            bres = res("PO", bank)
                            acquire(P, "te", reads=[R["xn2tP"], R["wqkv_b"]],
                                    writes=[bres])
                            for dcc in range(DC):
                                P.op("te", "matmul", PO[bank][:, 0:512],
                                     xn2tP[:, dcc * 256 + blk * 128:
                                           dcc * 256 + blk * 128 + 128],
                                     wqkv_b[:, dcc * 3072 + 2048 + rnd * 512:
                                            dcc * 3072 + 2048 + rnd * 512 + 512],
                                     start=(dcc == 0), stop=(dcc == DC - 1),
                                     inc=(dcc == DC - 1))
                            commit(P, "te", f"vm{j}_{blk}_{rnd}",
                                   reads=[R["xn2tP"]], writes=[bres])
                            tres = res("kvVP", q)
                            acquire(P, "ve", reads=[bres], writes=[tres])
                            P.wait_ck("ve", f"vm{j}_{blk}_{rnd}")
                            P.op("ve", "tensor_copy",
                                 kvVP[q][:, blk * 1024 + rnd * 512:
                                         blk * 1024 + rnd * 512 + 512],
                                 PO[bank][:, 0:512], inc=True)
                            commit(P, "ve", f"evv{j}_{blk}_{rnd}",
                                   reads=[bres], writes=[tres])
                            yield None
                    emit_tp(j, PTb0, R["PTb0"], xnb0, R["xnb0"],
                            xntP[q], res("xntP", q), 0, idb[:], f"xn_{B0}")
                    yield None
                    emit_tp(j, PTb0, R["PTb0"], xnb1, R["xnb1"],
                            xntP[q], res("xntP", q), 128, idb[:], f"xn_{B1}")
                    yield None
                    for blk in range(2):
                        for half in range(2):
                            emit_rt(2 * j + blk, half, f"{2 * j + blk}_{half}")
                            yield None

                def emit_halo_pre():
                    acquire(P, "te", reads=[R["xn2b1"]], writes=[R["PTb0"]])
                    P.wait_ck("te", "pcast")
                    for dcc in range(DC):
                        P.op("te", "transpose",
                             PTb0[:, dcc * 128:(dcc + 1) * 128],
                             xn2b1[:, dcc * 128:(dcc + 1) * 128], idb[:],
                             inc=(dcc == DC - 1))
                    commit(P, "te", "tph", reads=[R["xn2b1"]],
                           writes=[R["PTb0"]])
                    acquire(P, "ve", reads=[R["PTb0"]], writes=[R["xn2tP"]])
                    P.wait_ck("ve", "tph")
                    P.op("ve", "tensor_copy",
                         bass.AP(xn2tP, 128, [[2048, 128], [256, 8], [1, 128]]),
                         bass.AP(PTb0, 0, [[1024, 128], [128, 8], [1, 128]]),
                         inc=True)
                    commit(P, "ve", "tpeh", reads=[R["PTb0"]],
                           writes=[R["xn2tP"]])
                    for L in range(2):
                        bank = L % 2
                        bres = res("PO", bank)
                        acquire(P, "te", reads=[R["xn2tP"], R["wqkv_b"]],
                                writes=[bres])
                        for mi in range(4):
                            m = 8 + L * 4 + mi
                            for dcc in range(DC):
                                last = (mi == 3 and dcc == DC - 1)
                                P.op("te", "matmul",
                                     PO[bank][:, mi * 128:(mi + 1) * 128],
                                     wqkv_b[:, dcc * 3072 + m * 128:
                                            dcc * 3072 + m * 128 + 128],
                                     xn2tP[:, dcc * 256 + 128:
                                           dcc * 256 + 256],
                                     start=(mi == 0 and dcc == 0),
                                     stop=last, inc=last)
                        commit(P, "te", f"qkh_{L}", reads=[R["xn2tP"]],
                               writes=[bres])
                        acquire(P, "ve", reads=[bres], writes=[res("kvKP", 0)])
                        P.wait_ck("ve", f"qkh_{L}")
                        P.op("ve", "tensor_copy",
                             bass.AP(kvKP[0], L * 4 * 384,
                                     [[3072, 128], [384, 4], [1, 128]]),
                             bass.AP(PO[bank], 0,
                                     [[512, 128], [128, 4], [1, 128]]),
                             inc=True)
                        commit(P, "ve", f"eqkh_{L}", reads=[bres],
                               writes=[res("kvKP", 0)])
                    for rnd in range(2):
                        bank = rnd
                        bres = res("PO", bank)
                        acquire(P, "te", reads=[R["xn2tP"], R["wqkv_b"]],
                                writes=[bres])
                        for dcc in range(DC):
                            P.op("te", "matmul", PO[bank][:, 0:512],
                                 xn2tP[:, dcc * 256 + 128: dcc * 256 + 256],
                                 wqkv_b[:, dcc * 3072 + 2048 + rnd * 512:
                                        dcc * 3072 + 2048 + rnd * 512 + 512],
                                 start=(dcc == 0), stop=(dcc == DC - 1),
                                 inc=(dcc == DC - 1))
                        commit(P, "te", f"vmh_{rnd}", reads=[R["xn2tP"]],
                               writes=[bres])
                        acquire(P, "ve", reads=[bres], writes=[res("kvVP", 1)])
                        P.wait_ck("ve", f"vmh_{rnd}")
                        P.op("ve", "tensor_copy",
                             kvVP[1][:, 1024 + rnd * 512:
                                     1024 + rnd * 512 + 512],
                             PO[bank][:, 0:512], inc=True)
                        commit(P, "ve", f"evvh_{rnd}", reads=[bres],
                               writes=[res("kvVP", 1)])

                def emit_heads(b, feeder, cap, fstart=0):
                    q = (b // 2) % 2
                    p = b % 2
                    mask = m0b if b == 0 else mrb
                    NPH = NH // 2

                    def st_scores(g):
                        gb = g % 2
                        pr = (2 * g) % 4
                        sb = res("PS", gb)
                        acquire(P, "te", reads=[res("qtP", q), res("kvKP", q)],
                                writes=[sb])
                        if g == 0:
                            P.wait_ck("te", "pcast")
                        for i in range(2):
                            h = 2 * g + i
                            po = i * 64
                            P.op("te", "matmul",
                                 PS[gb][:, i * 256:(i + 1) * 256],
                                 idb[:], mask[:, i * 256:(i + 1) * 256],
                                 start=True, stop=False)
                            P.op("te", "matmul",
                                 PS[gb][:, i * 256:(i + 1) * 256],
                                 qtP[q][po:po + 64,
                                        g * 256 + p * 128: g * 256 + p * 128 + 128],
                                 kvKP[q][po:po + 64,
                                         g * 384 + p * 128: g * 384 + p * 128 + 256],
                                 start=False, stop=True, inc=(i == 1))
                        commit(P, "te", f"sc{b}_{g}",
                               reads=[res("qtP", q), res("kvKP", q)],
                               writes=[sb])

                        acquire(P, "ve", reads=[sb], writes=[res("negmx", pr)])
                        P.wait_ck("ve", f"sc{b}_{g}")
                        P.op("ve", "reduce_max", negmx[:, pr:pr + 2],
                             bass.AP(PS[gb], 0, [[512, 128], [256, 2], [1, 256]]),
                             AX, negate=True, inc=True)
                        commit(P, "ve", f"mx{b}_{g}", reads=[sb],
                               writes=[res("negmx", pr)])

                        acquire(P, "se", reads=[sb, res("negmx", pr)],
                                writes=[res("ebP", gb), res("ssum", pr)])
                        for i in range(2):
                            P.op("se", "activation",
                                 ebP[gb][:, i * 256:(i + 1) * 256],
                                 PS[gb][:, i * 256:(i + 1) * 256],
                                 AF.Exp, bias=negmx[:, pr + i:pr + i + 1],
                                 accum_out=ssum[:, pr + i:pr + i + 1],
                                 inc=(i == 1))
                        commit(P, "se", f"ex{b}_{g}", reads=[sb],
                               writes=[res("ebP", gb), res("ssum", pr)])

                    def st_dg(g):
                        pr = (2 * g) % 4
                        acquire(P, "ve", reads=[res("ssum", pr)],
                                writes=[res("srec", pr), res("dgb", pr),
                                        res("dgb", pr + 1)])
                        P.wait_ck("ve", f"ex{b}_{g}")
                        P.op("ve", "reciprocal", srec[:, pr:pr + 2],
                             ssum[:, pr:pr + 2])
                        P.op("ve", "tensor_scalar", dgb[pr][:], idb[:],
                             srec[:, pr:pr + 1], None, ALU.mult)
                        P.op("ve", "tensor_scalar", dgb[pr + 1][:], idb[:],
                             srec[:, pr + 1:pr + 2], None, ALU.mult, inc=True)
                        commit(P, "ve", f"dg{b}_{g}",
                               reads=[res("ssum", pr)],
                               writes=[res("srec", pr), res("dgb", pr),
                                       res("dgb", pr + 1)])

                    def st_pt(g):
                        gb = g % 2
                        pr = (2 * g) % 4
                        pb_ = res("PP", gb)
                        acquire(P, "te", reads=[res("ebP", gb), res("dgb", pr),
                                                res("dgb", pr + 1)],
                                writes=[pb_])
                        for i in range(2):
                            P.op("te", "matmul",
                                 PP[gb][:, i * 256: i * 256 + 128],
                                 ebP[gb][:, i * 256: i * 256 + 128],
                                 dgb[pr + i][:], start=(i == 0), stop=False)
                            P.op("te", "matmul",
                                 PP[gb][:, i * 256 + 128: i * 256 + 256],
                                 ebP[gb][:, i * 256 + 128: i * 256 + 256],
                                 dgb[pr + i][:], start=False, stop=(i == 1),
                                 inc=(i == 1))
                        commit(P, "te", f"pt{b}_{g}",
                               reads=[res("ebP", gb), res("dgb", pr),
                                      res("dgb", pr + 1)],
                               writes=[pb_])

                        acquire(P, "se", reads=[pb_], writes=[res("ptbP", gb)])
                        P.wait_ck("se", f"pt{b}_{g}")
                        P.op("se", "activation", ptbP[gb][:], PP[gb][:, 0:512],
                             AF.Copy, inc=True)
                        commit(P, "se", f"pte{b}_{g}", reads=[pb_],
                               writes=[res("ptbP", gb)])

                    def st_pv(g):
                        gb = g % 2
                        pb_ = res("PV", gb)
                        if p == 0:
                            vprev_t = kvVP[1 - q]
                            vprev_res = res("kvVP", 1 - q)
                            vprev_off = 1024
                        else:
                            vprev_t = kvVP[q]
                            vprev_res = res("kvVP", q)
                            vprev_off = 0
                        acquire(P, "te", reads=[res("kvVP", q), vprev_res,
                                                res("ptbP", gb)],
                                writes=[pb_])
                        for i in range(2):
                            h = 2 * g + i
                            po = i * 64
                            tp = (0, po) if po else None
                            P.op("te", "matmul", PV[gb][po:po + 64, 0:128],
                                 vprev_t[:, vprev_off + h * 64:
                                         vprev_off + h * 64 + 64],
                                 ptbP[gb][:, i * 256: i * 256 + 128],
                                 start=True, stop=False, tile_position=tp)
                            P.op("te", "matmul", PV[gb][po:po + 64, 0:128],
                                 kvVP[q][:, p * 1024 + h * 64:
                                         p * 1024 + h * 64 + 64],
                                 ptbP[gb][:, i * 256 + 128: i * 256 + 256],
                                 start=False, stop=True, tile_position=tp,
                                 inc=(i == 1))
                        commit(P, "te", f"pv{b}_{g}",
                               reads=[res("kvVP", q), vprev_res,
                                      res("ptbP", gb)],
                               writes=[pb_])

                        acquire(P, "se", reads=[pb_], writes=[res("otP", q)])
                        P.wait_ck("se", f"pv{b}_{g}")
                        P.op("se", "activation",
                             otP[q][:, g * 256 + p * 128: g * 256 + p * 128 + 128],
                             PV[gb][:, 0:128], AF.Copy, inc=True)
                        commit(P, "se", f"oe{b}_{g}", reads=[pb_],
                               writes=[res("otP", q)])

                    for it in range(NPH + 3):
                        if 2 <= it <= NPH + 1:
                            st_pt(it - 2)
                        if it >= 3:
                            st_pv(it - 3)
                        if it < NPH:
                            st_scores(it)
                        if 1 <= it <= NPH:
                            st_dg(it - 1)
                        if it >= fstart:
                            feeder.feed(cap)


                def emit_wo_stores(j):
                    q = j % 2
                    for L in range(4):
                        wb_ = res("PP", L % 2)
                        acquire(P, "te", reads=[res("otP", q)], writes=[wb_])
                        if L == 0:
                            P.wait_ck("te", "pcast")
                        for di in range(2):
                            dcc = 2 * L + di
                            for oc in range(DC):
                                last = (di == 1 and oc == DC - 1)
                                P.op("te", "matmul",
                                     PP[L % 2][:, di * 256:(di + 1) * 256],
                                     wo_b[:, oc * 1024 + dcc * 128:
                                          oc * 1024 + dcc * 128 + 128],
                                     otP[q][:, oc * 256:(oc + 1) * 256],
                                     start=(di == 0 and oc == 0),
                                     stop=last, inc=last)
                        commit(P, "te", f"wo{j}_{L}", reads=[res("otP", q)],
                               writes=[wb_])

                        acquire(P, "ve", reads=[wb_, res("rtP", q)],
                                writes=[res("routP", q)])
                        P.wait_ck("ve", f"wo{j}_{L}")
                        P.op("ve", "scalar_tensor_tensor",
                             routP[q][:, L * 512:(L + 1) * 512],
                             PP[L % 2][:, 0:512], 1.0,
                             rtP[q][:, L * 512:(L + 1) * 512],
                             ALU.mult, ALU.add, inc=True)
                        commit(P, "ve", f"ewo{j}_{L}",
                               reads=[wb_, res("rtP", q)], writes=[res("routP", q)])

                    acquire(P, "gp", reads=[res("routP", q), res("xntP", q)])
                    P.dma("gp", f"sst{j % 2}",
                          scr_r.ap().rearrange("(a b) c -> b a c", b=128)
                          [:, :, j * 256:(j + 1) * 256],
                          bass.AP(routP[q], 0, [[2048, 128], [256, 8], [1, 256]]))
                    P.dma("gp", f"sst{j % 2}",
                          scr_xn.ap().rearrange("(a b) c -> b a c", b=128)
                          [:, :, j * 256:(j + 1) * 256],
                          bass.AP(xntP[q], 0, [[2048, 128], [256, 8], [1, 256]]))
                    commit_dma(P, f"sst{j % 2}", f"str{j}",
                               reads=[res("routP", q), res("xntP", q)])

                class Feeder:
                    def __init__(self, gen):
                        self.gen = gen
                        self.count = 0
                        self.done = gen is None

                    def feed(self, cap):
                        if not self.done and self.count < cap:
                            try:
                                next(self.gen)
                                self.count += 1
                            except StopIteration:
                                self.done = True

                    def drain(self):
                        if not self.done:
                            for _ in self.gen:
                                pass
                            self.done = True

                emit_prep(-1)
                emit_prep(0)
                emit_halo_pre()
                emit_prep(1)
                Feeder(emit_pre_pair(0)).drain()
                for j in range(NPAIR):
                    gens = [emit_prep_gen(b2) for b2 in (2 * j + 2, 2 * j + 3)
                            if b2 < NBLK]
                    alive = True
                    while alive:
                        alive = False
                        for gg in gens:
                            try:
                                next(gg)
                                alive = True
                            except StopIteration:
                                pass
                    if j > 0:
                        emit_wo_stores(j - 1)
                    if j < NPAIR - 1:
                        q = j % 2
                        acquire(P, "ve", reads=[res("kvKP", q)],
                                writes=[res("kvKP", 1 - q)])
                        P.op("ve", "tensor_copy",
                             bass.AP(kvKP[1 - q], 0,
                                     [[3072, 128], [384, 8], [1, 128]]),
                             bass.AP(kvKP[q], 256,
                                     [[3072, 128], [384, 8], [1, 128]]),
                             inc=True)
                        commit(P, "ve", f"kvc{j}", reads=[res("kvKP", q)],
                               writes=[res("kvKP", 1 - q)])
                    feeder = Feeder(emit_pre_pair(j + 1)
                                    if j + 1 < NPAIR else None)
                    emit_heads(2 * j, feeder, 10)
                    emit_heads(2 * j + 1, feeder, 14)
                    feeder.drain()
                emit_wo_stores(NPAIR - 1)


                for e in ("te", "ve", "se"):
                    P.ckpt(e, f"Aend_{e}")
                if dbgst is not None:
                    for e in ("ve", "se"):
                        P.wait_ck("gp", f"Aend_{e}")
                    P.dma("gp", "slt", dbgst.ap(), stats[:])
                    commit_dma(P, "slt", "dbgstd")
                P.dma_ckpt("sst0", "Ast0")
                P.dma_ckpt("sst1", "Ast1")

            Pc = Prog("count", ck)
            emit_A(Pc)
            totals_A = (dict(Pc.cnt), dict(Pc.dcnt))
            track_A = ({e: {tn: dict(d) for tn, d in sh.items()}
                        for e, sh in Pc.shadow.items()}, dict(Pc.self_hw))

            with nc.Block() as blk:
                @blk.tensor
                def _(e):
                    Px = Prog("emit", ck, "te", {"te": e}, esem, dsem)
                    emit_A(Px)
                    assert Px.cnt == totals_A[0]

                @blk.vector
                def _(e):
                    Px = Prog("emit", ck, "ve", {"ve": e}, esem, dsem)
                    emit_A(Px)

                @blk.scalar
                def _(e):
                    Px = Prog("emit", ck, "se", {"se": e}, esem, dsem)
                    emit_A(Px)

                @blk.sync
                def _(e):
                    Px = Prog("emit", ck, "sy", {"sy": e}, esem, dsem)
                    emit_A(Px)

                @blk.gpsimd
                def _(e):
                    Px = Prog("emit", ck, "gp", {"gp": e}, esem, dsem)
                    emit_A(Px)
                    assert Px.dcnt == totals_A[1]

        with contextlib.ExitStack() as stB:
            def sbB(name, shape, dtype):
                return stB.enter_context(nc.sbuf_tensor(name, shape, dtype))

            def pbB(name, shape, dtype):
                return stB.enter_context(nc.psum_tensor(name, shape, dtype))

            w1t_b = sbB("w1t_b", [128, DC * HF], BF16)
            w2t_b = sbB("w2t_b", [128, HCN * D], BF16)
            xnt20 = sbB("xnt20", [128, DC * TB], BF16)
            xnt21 = sbB("xnt21", [128, DC * TB], BF16)
            rtb0 = sbB("rtb0", [128, DC * TB], F32)
            rtb1 = sbB("rtb1", [128, DC * TB], F32)
            hbf0 = sbB("hbf0", [128, TB], BF16)
            hbf1 = sbB("hbf1", [128, TB], BF16)
            u_bf = sbB("u_bf", [4, TB], BF16)
            t2_bf = sbB("t2_bf", [4, 4 * TB], BF16)
            outb0 = sbB("outb0", [128, DC * TB], F32)
            outb1 = sbB("outb1", [128, DC * TB], F32)
            H0 = pbB("H0", [128, 512], F32)
            H1 = pbB("H1", [128, 512], F32)
            T2a = pbB("T2a", [128, 512], F32)
            T2b = pbB("T2b", [128, 512], F32)
            OUT0 = pbB("OUT0", [128, 512], F32)
            OUT1 = pbB("OUT1", [128, 512], F32)
            OUT2 = pbB("OUT2", [128, 512], F32)
            OUT3 = pbB("OUT3", [128, 512], F32)

            xnt2 = (xnt20, xnt21)
            rtb = (rtb0, rtb1)
            hbf = (hbf0, hbf1)
            outb = (outb0, outb1)
            HB = (H0, H1)
            T2 = (T2a, T2b)
            OUT = (OUT0, OUT1, OUT2, OUT3)

            def emit_B(P):
                R = {n: Res() for n in (
                    "xnt20", "xnt21", "rtb0", "rtb1", "hbf0", "hbf1",
                    "u_bf", "t2_bf", "outb0", "outb1",
                    "H0", "H1", "T2a", "T2b", "OUT0", "OUT1", "OUT2", "OUT3",
                )}

                def res(base, i=None):
                    return R[base + ("" if i is None else str(i))]

                for e in ("te", "ve", "se", "sy", "gp"):
                    for o in ("te", "ve", "se"):
                        if o != e:
                            P.wait_ck(e, f"Aend_{o}")
                    for n in ("Ast0", "Ast1", "pload", "pcast",
                              "evd2", "oddte"):
                        P.wait_ck(e, n)

                def w1_chunk(c):
                    P.dma("gp", f"swA{c % 8}",
                          w1t_b[:, c * 1024:(c + 1) * 1024],
                          w1td.ap().rearrange("(a b) c -> b a c", b=128)
                          [:, :, c * 128:(c + 1) * 128])

                def w2_chunk(c):
                    P.dma("gp", f"swB{c % 8}",
                          w2t_b[:, c * 1024:(c + 1) * 1024],
                          w2td.ap().rearrange("(a b) c -> b a c", b=128)
                          [:, c:c + 1, :])

                for c in range(8):
                    w1_chunk(c)
                    w2_chunk(c)

                for s in range(NST):
                    q = s % 2
                    t0 = s * TB

                    acquire(P, "sy", writes=[res("xnt2", q), res("rtb", q)])
                    P.dma("sy", f"sbi{q}",
                          bass.AP(xnt2[q], 0, [[2048, 128], [256, 8], [1, 256]]),
                          scr_xn.ap().rearrange("(a b) c -> b a c", b=128)
                          [:, :, t0:t0 + TB])
                    P.dma("sy", f"sbi{q}",
                          bass.AP(rtb[q], 0, [[2048, 128], [256, 8], [1, 256]]),
                          scr_r.ap().rearrange("(a b) c -> b a c", b=128)
                          [:, :, t0:t0 + TB])
                    commit_dma(P, f"sbi{q}", f"bl{s}",
                               writes=[res("xnt2", q), res("rtb", q)])

                    acquire(P, "te", reads=[res("xnt2", q)], writes=[R["T2a"]])
                    P.wait_ck("te", f"bl{s}")
                    for dcc in range(DC):
                        P.op("te", "matmul", T2a[0:4, 0:256],
                             oddtb[:, dcc * 8: dcc * 8 + 4],
                             xnt2[q][:, dcc * 256:(dcc + 1) * 256],
                             start=(dcc == 0), stop=(dcc == DC - 1),
                             inc=(dcc == DC - 1))
                    commit(P, "te", f"umm{s}", reads=[res("xnt2", q)],
                           writes=[R["T2a"]])
                    acquire(P, "ve", reads=[R["T2a"]], writes=[R["u_bf"]])
                    P.wait_ck("ve", f"umm{s}")
                    P.op("ve", "tensor_scalar", u_bf[:], T2a[0:4, 0:256],
                         sc8[0:4, 0:1], None, ALU.mult, inc=True)
                    commit(P, "ve", f"uev{s}", reads=[R["T2a"]],
                           writes=[R["u_bf"]])

                    def emit_w1(hcc):
                            hp = hcc % 2
                            r = hcc // 8
                            qc2 = hcc % 8
                            hb_ = res("H", hp)
                            acquire(P, "te", reads=[res("xnt2", q), R["u_bf"]],
                                    writes=[hb_])
                            if s == 0:
                                P.wait_dma("te", f"swA{hcc % 8}",
                                           16 * (hcc // 8 + 1))
                            for dcc in range(DC):
                                P.op("te", "matmul", HB[hp][:, 0:256],
                                     w1t_b[:, hcc * 1024 + dcc * 128:
                                           hcc * 1024 + dcc * 128 + 128],
                                     xnt2[q][:, dcc * 256:(dcc + 1) * 256],
                                     start=(dcc == 0), stop=False)
                            P.op("te", "matmul", HB[hp][:, 0:256],
                                 ev1zb[0:4, hcc * 128:(hcc + 1) * 128],
                                 u_bf[0:4, :], start=False, stop=True, inc=True)
                            commit(P, "te", f"w1{s}_{hcc}",
                                   reads=[res("xnt2", q), R["u_bf"]], writes=[hb_])
                            if s == 0 and hcc + 8 < HCN:
                                P.wait_ck("gp", f"w1{s}_{hcc}")
                                w1_chunk(hcc + 8)


                    def emit_gt(hcc):
                            hp = hcc % 2
                            r = hcc // 8
                            qc2 = hcc % 8
                            hb_ = res("H", hp)
                            acquire(P, "se", reads=[hb_], writes=[res("hbf", hp)])
                            P.wait_ck("se", f"w1{s}_{hcc}")
                            P.op("se", "activation", hbf[hp][:], HB[hp][:, 0:256],
                                 AF.Gelu, bias=b1_sb[:, hcc:hcc + 1], inc=True)
                            commit(P, "se", f"ge{s}_{hcc}", reads=[hb_],
                                   writes=[res("hbf", hp)])

                            t2bank = T2[r // 2]
                            t2res = res("T2", ("a", "b")[r // 2])
                            acquire(P, "te", reads=[res("hbf", hp)],
                                    writes=[t2res] + [res("OUT", i) for i in range(4)])
                            P.wait_ck("te", f"ge{s}_{hcc}")
                            if s == 0:
                                P.wait_dma("te", f"swB{hcc % 8}",
                                           16 * (hcc // 8 + 1))
                            P.op("te", "matmul",
                                 t2bank[0:4, (r % 2) * 256:(r % 2) * 256 + 256],
                                 oddtb[:, qc2 * 8 + 4: qc2 * 8 + 8],
                                 hbf[hp][:, 0:256],
                                 start=(qc2 == 0 and r % 2 == 0),
                                 stop=(qc2 == 7 and r % 2 == 1))
                            for oc in range(DC):
                                last = (oc == DC - 1)
                                P.op("te", "matmul",
                                     OUT[oc // 2][:, (oc % 2) * 256:(oc % 2) * 256 + 256],
                                     w2t_b[:, hcc * 1024 + oc * 128:
                                           hcc * 1024 + oc * 128 + 128],
                                     hbf[hp][:, 0:256],
                                     start=(hcc == 0 and oc % 2 == 0),
                                     stop=False, inc=last)
                            commit(P, "te", f"t2w2{s}_{hcc}",
                                   reads=[res("hbf", hp)],
                                   writes=[t2res] + [res("OUT", i) for i in range(4)])
                            if s == 0 and hcc + 8 < HCN:
                                P.wait_ck("gp", f"t2w2{s}_{hcc}")
                                w2_chunk(hcc + 8)

                    emit_w1(0)
                    for hcc in range(HCN):
                        if hcc + 1 < HCN:
                            emit_w1(hcc + 1)
                        emit_gt(hcc)

                    acquire(P, "ve", reads=[R["T2a"], R["T2b"]],
                            writes=[R["t2_bf"]])
                    P.wait_ck("ve", f"t2w2{s}_{HCN - 1}")
                    P.op("ve", "tensor_scalar", t2_bf[0:4, 0:512],
                         T2a[0:4, 0:512], sc8[0:4, 0:1], None, ALU.mult)
                    P.op("ve", "tensor_scalar", t2_bf[0:4, 512:1024],
                         T2b[0:4, 0:512], sc8[0:4, 0:1], None, ALU.mult,
                         inc=True)
                    commit(P, "ve", f"t2e{s}", reads=[R["T2a"], R["T2b"]],
                           writes=[R["t2_bf"]])

                    acquire(P, "te", reads=[R["t2_bf"]],
                            writes=[res("OUT", i) for i in range(4)])
                    P.wait_ck("te", f"t2e{s}")
                    for oc in range(DC):
                        for r in range(4):
                            last = (oc == DC - 1 and r == 3)
                            P.op("te", "matmul",
                                 OUT[oc // 2][:, (oc % 2) * 256:(oc % 2) * 256 + 256],
                                 ev2zb[0:4, r * 1024 + oc * 128:
                                       r * 1024 + oc * 128 + 128],
                                 t2_bf[0:4, r * 256:(r + 1) * 256],
                                 start=False,
                                 stop=(r == 3 and oc % 2 == 1), inc=last)
                    commit(P, "te", f"corr{s}", reads=[R["t2_bf"]],
                           writes=[res("OUT", i) for i in range(4)])

                    acquire(P, "ve",
                            reads=[res("rtb", q)] + [res("OUT", i) for i in range(4)],
                            writes=[res("outb", q)])
                    P.wait_ck("ve", f"corr{s}")
                    for oc in range(DC):
                        P.op("ve", "scalar_tensor_tensor",
                             outb[q][:, oc * 256:(oc + 1) * 256],
                             OUT[oc // 2][:, (oc % 2) * 256:(oc % 2) * 256 + 256],
                             b2_sb[:, oc:oc + 1],
                             rtb[q][:, oc * 256:(oc + 1) * 256],
                             ALU.add, ALU.add, inc=(oc == DC - 1))
                    commit(P, "ve", f"oute{s}",
                           reads=[res("rtb", q)] + [res("OUT", i) for i in range(4)],
                           writes=[res("outb", q)])

                    acquire(P, "gp", reads=[res("outb", q)])
                    P.dma("gp", f"sso{q}",
                          outd.ap().rearrange("(a b) c -> b a c", b=128)
                          [:, :, t0:t0 + TB],
                          bass.AP(outb[q], 0, [[2048, 128], [256, 8], [1, 256]]))
                    commit_dma(P, f"sso{q}", f"os{s}", reads=[res("outb", q)])

                P.wait_dma("gp", "sso0", P.dcnt["sso0"])
                P.wait_dma("gp", "sso1", P.dcnt["sso1"])

            Pc2 = Prog("count", ck)
            Pc2.cnt = dict(totals_A[0])
            Pc2.dcnt = dict(totals_A[1])
            Pc2.shadow = {e: {tn: dict(d) for tn, d in sh.items()}
                          for e, sh in track_A[0].items()}
            Pc2.self_hw = dict(track_A[1])
            emit_B(Pc2)
            totals_B = (dict(Pc2.cnt), dict(Pc2.dcnt))

            def mkprog(eng, e):
                Px = Prog("emit", ck, eng, {eng: e}, esem, dsem)
                Px.cnt = dict(totals_A[0])
                Px.dcnt = dict(totals_A[1])
                Px.shadow = {e2: {tn: dict(d) for tn, d in sh.items()}
                             for e2, sh in track_A[0].items()}
                Px.self_hw = dict(track_A[1])
                return Px

            with nc.Block() as blk2:
                @blk2.tensor
                def _(e):
                    Px = mkprog("te", e)
                    emit_B(Px)
                    assert Px.cnt == totals_B[0]

                @blk2.vector
                def _(e):
                    emit_B(mkprog("ve", e))

                @blk2.scalar
                def _(e):
                    emit_B(mkprog("se", e))

                @blk2.sync
                def _(e):
                    emit_B(mkprog("sy", e))

                @blk2.gpsimd
                def _(e):
                    Px = mkprog("gp", e)
                    emit_B(Px)
                    assert Px.dcnt == totals_B[1]

    return nc


def _masks():
    i = np.arange(128)[:, None]
    j = np.arange(256)[None, :]
    base = (j > i) & (j <= i + 128)
    first = base & (j >= 128)
    mb = np.where(base, 0.0, -1e30).astype(np.float32)
    mf = np.where(first, 0.0, -1e30).astype(np.float32)
    return mb, mf


def kernel(L0_attn_x, L0_attn_g, L0_attn_Wqkv, L0_attn_Wo,
           L0_ffn_W1, L0_ffn_b1, L0_ffn_W2, L0_ffn_b2, L0_ffn_scale):
    X = np.ascontiguousarray(np.asarray(L0_attn_x, dtype=np.float32))
    g = np.asarray(L0_attn_g, dtype=np.float32)
    Wqkv = np.ascontiguousarray(np.asarray(L0_attn_Wqkv, dtype=np.float32))
    Wo = np.ascontiguousarray(np.asarray(L0_attn_Wo, dtype=np.float32))
    W1t = np.ascontiguousarray(np.asarray(L0_ffn_W1, dtype=np.float32).T)
    W2t = np.ascontiguousarray(np.asarray(L0_ffn_W2, dtype=np.float32).T)
    b1 = np.asarray(L0_ffn_b1, dtype=np.float32)
    b2 = np.asarray(L0_ffn_b2, dtype=np.float32)
    sc = np.asarray(L0_ffn_scale, dtype=np.float32).reshape(1, 1)

    gm = np.ascontiguousarray(g.reshape(8, 128).T)
    b1m = np.ascontiguousarray(b1.reshape(32, 128).T)
    b2m = np.ascontiguousarray(b2.reshape(8, 128).T)
    ident = np.eye(128, dtype=np.float32)
    mb, mf = _masks()
    mb = np.ascontiguousarray(np.hstack([mb, mb]))
    mf = np.ascontiguousarray(np.hstack([mf, mf]))

    nc = build_nc()
    in_maps = []
    for c in range(8):
        b = c // 4
        t0 = (c % 4) * TC
        xh = (np.zeros((WIN, D), np.float32) if t0 == 0
              else np.ascontiguousarray(X[b, t0 - WIN:t0]))
        in_maps.append({
            "x": np.ascontiguousarray(X[b, t0:t0 + TC]),
            "xh": xh,
            "xle": np.ascontiguousarray(X[b, 1:17:2]),
            "xlo": np.ascontiguousarray(X[b, 2:17:2]),
            "wqkv": Wqkv, "wo": Wo, "w1t": W1t, "w2t": W2t,
            "gm": gm, "b1m": b1m, "b2m": b2m, "scv": sc,
            "mask0": (mf if t0 == 0 else mb), "maskr": mb,
            "ident": ident,
        })

    res = run_bass_kernel_spmd(nc, in_maps, core_ids=list(range(8)))
    out = np.empty((2, 4 * TC, D), np.float32)
    for c in range(8):
        b = c // 4
        t0 = (c % 4) * TC
        out[b, t0:t0 + TC, :] = res.results[c]["out"].T
    return out


# version 35 (final)
# speedup vs baseline: 1.0239x; 1.0093x over previous
import numpy as np
import concourse.bass as bass
import concourse.mybir as mybir
from concourse.bass_utils import run_bass_kernel_spmd

F32 = mybir.dt.float32
BF16 = mybir.dt.bfloat16
AF = mybir.ActivationFunctionType
ALU = mybir.AluOpType
AX = mybir.AxisListType.X

D = 1024
DC = 8
NH = 16
DH = 64
WIN = 128
TC = 4096
NBLK = TC // WIN
HF = 4096
HCN = 32
TB = 256
NST = TC // TB

ENGS = ("te", "ve", "se", "sy", "gp")
INTERLEAVE = True
INTERLEAVE_K = 16
DSEMS = ("sx0", "sx1", "swa", "swg", "slt", "sst0", "sst1",
         "swb1", "swb2", "sbi0", "sbi1", "sso0", "sso1") + tuple(
    f"swA{i}" for i in range(8)) + tuple(f"swB{i}" for i in range(8))


class Res:
    __slots__ = ("w", "r")

    def __init__(self):
        self.w = None
        self.r = []


def _rect(x):
    if not isinstance(x, bass.AP):
        return None
    t = x.tensor
    ncol = int(t.shape[-1])
    off = int(x.offset)
    p_ext = 0
    c_ext = 0
    for pr in x.ap:
        stride, n = int(pr[0]), int(pr[1])
        if n <= 1:
            continue
        if stride >= ncol and stride % ncol == 0:
            p_ext += (stride // ncol) * (n - 1)
        else:
            c_ext += stride * (n - 1)
    p0 = off // ncol
    c0 = off % ncol
    return (t.name, p0, p0 + p_ext + 1, c0, c0 + c_ext + 1)


_TRACKED = ("ve", "se")


class Prog:
    def __init__(self, mode, ck, filt=None, engs=None, esem=None, dsem=None):
        self.mode = mode
        self.ck = ck
        self.filt = filt
        self.engs = engs or {}
        self.esem = esem or {}
        self.dsem = dsem or {}
        self.cnt = {e: 0 for e in ENGS}
        self.dcnt = {s: 0 for s in DSEMS}
        self.shadow = {e: {} for e in _TRACKED}
        self.self_hw = {e: 0 for e in _TRACKED}

    def active(self, eng):
        return self.mode == "emit" and self.filt == eng

    def wait_ck(self, eng, name):
        if name is None:
            return
        kind, key, val = self.ck[name]
        if kind == "eng":
            if key == eng or val == 0:
                return
            if self.active(eng):
                self.engs[eng].wait_ge(self.esem[key], val)
        else:
            if val == 0:
                return
            if self.active(eng):
                self.engs[eng].wait_ge(self.dsem[key], val)

    def wait_dma(self, eng, dsname, val):
        if val and self.active(eng):
            self.engs[eng].wait_ge(self.dsem[dsname], val)

    def op(self, eng, meth, *a, inc=False, **k):
        if eng in _TRACKED:
            wr, rd = [], []
            for i, x in enumerate(a):
                r = _rect(x)
                if r is not None:
                    (wr if i == 0 else rd).append(r)
            for key in ("bias", "scale"):
                r = _rect(k.get(key))
                if r is not None:
                    rd.append(r)
            r = _rect(k.get("accum_out"))
            if r is not None:
                wr.append(r)
            need = 0
            sh = self.shadow[eng]
            for (tn, p0, p1, c0, c1) in rd:
                for (isw, q0, q1, d0, d1), val in sh.get(tn, {}).items():
                    if (isw and val > need and p0 < q1 and q0 < p1
                            and c0 < d1 and d0 < c1):
                        need = val
            for (tn, p0, p1, c0, c1) in wr:
                for (isw, q0, q1, d0, d1), val in sh.get(tn, {}).items():
                    if (val > need and p0 < q1 and q0 < p1
                            and c0 < d1 and d0 < c1):
                        need = val
            if need > self.self_hw[eng]:
                self.self_hw[eng] = need
                if self.active(eng):
                    self.engs[eng].wait_ge(self.esem[eng], need)
            self.cnt[eng] += 1
            v = self.cnt[eng]
            for (tn, p0, p1, c0, c1) in wr:
                d = sh.setdefault(tn, {})
                d[(True, p0, p1, c0, c1)] = v
                if len(d) > 96:
                    hw = self.self_hw[eng]
                    for kk in [kk for kk, vv in d.items() if vv <= hw]:
                        del d[kk]
            for (tn, p0, p1, c0, c1) in rd:
                d = sh.setdefault(tn, {})
                d[(False, p0, p1, c0, c1)] = v
                if len(d) > 96:
                    hw = self.self_hw[eng]
                    for kk in [kk for kk, vv in d.items() if vv <= hw]:
                        del d[kk]
            ins = None
            if self.active(eng):
                ins = getattr(self.engs[eng], meth)(*a, **k)
                ins.then_inc(self.esem[eng], 1)
            return ins
        ins = None
        if self.active(eng):
            ins = getattr(self.engs[eng], meth)(*a, **k)
        if inc:
            self.cnt[eng] += 1
            if ins is not None:
                ins.then_inc(self.esem[eng], 1)
        return ins

    def dma(self, eng, dsname, out, in_, **kw):
        self.dcnt[dsname] += 16
        if self.active(eng):
            self.engs[eng].dma_start(out=out, in_=in_, **kw).then_inc(
                self.dsem[dsname], 16)

    def ckpt(self, eng, name):
        if self.mode == "count":
            assert name not in self.ck, name
            self.ck[name] = ("eng", eng, self.cnt[eng])

    def dma_ckpt(self, dsname, name):
        if self.mode == "count":
            assert name not in self.ck, name
            self.ck[name] = ("dma", dsname, self.dcnt[dsname])


def acquire(P, eng, reads=(), writes=()):
    names = []
    for x in reads:
        names.append(x.w)
    for x in writes:
        names.append(x.w)
        names.extend(x.r)
    for n in dict.fromkeys(names):
        P.wait_ck(eng, n)


def commit(P, eng, name, reads=(), writes=()):
    P.ckpt(eng, name)
    for x in reads:
        x.r.append(name)
    for x in writes:
        x.w = name
        x.r = []


def commit_dma(P, dsname, name, reads=(), writes=()):
    P.dma_ckpt(dsname, name)
    for x in reads:
        x.r.append(name)
    for x in writes:
        x.w = name
        x.r = []


def build_nc(debug=False):
    nc = bass.Bass("TRN2", target_bir_lowering=False)
    scr_kind = {"kind": "ExternalOutput"} if debug else {}

    xd = nc.dram_tensor("x", [TC, D], F32, kind="ExternalInput")
    xhd = nc.dram_tensor("xh", [WIN, D], F32, kind="ExternalInput")
    xled = nc.dram_tensor("xle", [8, D], F32, kind="ExternalInput")
    xlod = nc.dram_tensor("xlo", [8, D], F32, kind="ExternalInput")
    wqkvd = nc.dram_tensor("wqkv", [D, 3 * D], F32, kind="ExternalInput")
    wod = nc.dram_tensor("wo", [D, D], F32, kind="ExternalInput")
    w1td = nc.dram_tensor("w1t", [D, HF], F32, kind="ExternalInput")
    w2td = nc.dram_tensor("w2t", [HF, D], F32, kind="ExternalInput")
    gmd = nc.dram_tensor("gm", [128, 8], F32, kind="ExternalInput")
    b1md = nc.dram_tensor("b1m", [128, 32], F32, kind="ExternalInput")
    b2md = nc.dram_tensor("b2m", [128, 8], F32, kind="ExternalInput")
    scd = nc.dram_tensor("scv", [1, 1], F32, kind="ExternalInput")
    m0d = nc.dram_tensor("mask0", [128, 512], F32, kind="ExternalInput")
    mrd = nc.dram_tensor("maskr", [128, 512], F32, kind="ExternalInput")
    identd = nc.dram_tensor("ident", [128, 128], F32, kind="ExternalInput")
    scr_r = nc.dram_tensor("scr_r", [D, TC], F32, **scr_kind)
    dbgst = nc.dram_tensor("dbgst", [128, 80], F32, **scr_kind) if debug else None
    scr_xn = nc.dram_tensor("scr_xn", [D, TC], BF16, **scr_kind)
    outd = nc.dram_tensor("out", [D, TC], F32, kind="ExternalOutput")

    ck = {}

    import contextlib
    with contextlib.ExitStack() as stack:
        def sb(name, shape, dtype):
            return stack.enter_context(nc.sbuf_tensor(name, shape, dtype))

        def sem(name):
            return stack.enter_context(nc.semaphore(name))

        gm_sb = sb("gm_sb", [128, 8], F32)
        b1_sb = sb("b1_sb", [128, 32], F32)
        b2_sb = sb("b2_sb", [128, 8], F32)
        sc8 = sb("sc8", [8, 1], F32)
        idf = sb("idf", [128, 128], F32)
        idb = sb("idb", [128, 128], BF16)
        m0b = sb("m0b", [128, 512], BF16)
        mrb = sb("mrb", [128, 512], BF16)
        ev1zb = sb("ev1zb", [4, HF], BF16)
        ev2zb = sb("ev2zb", [4, HF], BF16)
        oddtb = sb("oddtb", [128, 64], BF16)
        stats = sb("stats", [128, 80], F32)
        negmx = sb("negmx", [128, 4], F32)
        ssum = sb("ssum", [128, 4], F32)
        srec = sb("srec", [128, 4], F32)
        s_te = sem("s_te")
        s_ve = sem("s_ve")
        s_se = sem("s_se")
        dsh = {n: sem("d_" + n) for n in DSEMS}

        esem = {"te": s_te, "ve": s_ve, "se": s_se}
        dsem = dsh

        with contextlib.ExitStack() as stA:
            def sa(name, shape, dtype):
                return stA.enter_context(nc.sbuf_tensor(name, shape, dtype))

            def pa(name, shape, dtype):
                return stA.enter_context(nc.psum_tensor(name, shape, dtype))

            wqkv_b = sa("wqkv_b", [128, DC * 3 * D], BF16)
            wo_b = sa("wo_b", [128, DC * D], BF16)

            xbuf0 = sa("xbuf0", [128, D], F32)
            xbuf1 = sa("xbuf1", [128, D], F32)
            xc0 = sa("xc0", [128, D], F32)
            xc1 = sa("xc1", [128, D], F32)
            xn2b0 = sa("xn2b0", [128, D], BF16)
            xn2b1 = sa("xn2b1", [128, D], BF16)
            xnb0 = sa("xnb0", [128, D], BF16)
            xnb1 = sa("xnb1", [128, D], BF16)
            r320 = sa("r320", [128, D], F32)
            r321 = sa("r321", [128, D], F32)
            xn2tP = sa("xn2tP", [128, 2048], BF16)
            xntP0 = sa("xntP0", [128, 2048], BF16)
            xntP1 = sa("xntP1", [128, 2048], BF16)
            xntP = (xntP0, xntP1)
            rtP0 = sa("rtP0", [128, 2048], F32)
            rtP1 = sa("rtP1", [128, 2048], F32)
            rtP = (rtP0, rtP1)
            qtP0 = sa("qtP0", [128, 2048], BF16)
            qtP1 = sa("qtP1", [128, 2048], BF16)
            kvKP0 = sa("kvKP0", [128, 3072], BF16)
            kvKP1 = sa("kvKP1", [128, 3072], BF16)
            kvVP0 = sa("kvVP0", [128, 2048], BF16)
            kvVP1 = sa("kvVP1", [128, 2048], BF16)
            ebP0 = sa("ebP0", [128, 512], BF16)
            ebP1 = sa("ebP1", [128, 512], BF16)
            ptbP0 = sa("ptbP0", [128, 512], BF16)
            ptbP1 = sa("ptbP1", [128, 512], BF16)
            dgb0 = sa("dgb0", [128, 128], BF16)
            dgb1 = sa("dgb1", [128, 128], BF16)
            dgb2 = sa("dgb2", [128, 128], BF16)
            dgb3 = sa("dgb3", [128, 128], BF16)
            otP0 = sa("otP0", [128, 2048], BF16)
            otP1 = sa("otP1", [128, 2048], BF16)
            routP0 = sa("routP0", [128, 2048], F32)
            routP1 = sa("routP1", [128, 2048], F32)
            routP = (routP0, routP1)
            PS0 = pa("PS0", [128, 512], F32)
            PS1 = pa("PS1", [128, 512], F32)
            PP0 = pa("PP0", [128, 512], F32)
            PP1 = pa("PP1", [128, 512], F32)
            PO0 = pa("PO0", [128, 512], F32)
            PO1 = pa("PO1", [128, 512], F32)
            PTb0 = pa("PTb0", [128, 1024], BF16)
            PV0 = pa("PV0", [128, 512], F32)
            PV1 = PV0

            xbuf = (xbuf0, xbuf1)
            xcb = (xc0, xc1)
            xn2b = (xn2b0, xn2b1)
            xnb = (xnb0, xnb1)
            r32 = (r320, r321)
            qtP = (qtP0, qtP1)
            kvKP = (kvKP0, kvKP1)
            kvVP = (kvVP0, kvVP1)
            otP = (otP0, otP1)
            ebP = (ebP0, ebP1)
            ptbP = (ptbP0, ptbP1)
            dgb = (dgb0, dgb1, dgb2, dgb3)

            PS = (PS0, PS1)
            PP = (PP0, PP1)
            PO = (PO0, PO1)
            PV = (PV0, PV1)

            def scol(slot, c, rows=128):
                return stats[0:rows, slot * 16 + c: slot * 16 + c + 1]

            def emit_A(P):
                R = {n: Res() for n in (
                    "xbuf0", "xbuf1", "xc0", "xc1", "xn2b0", "xn2b1",
                    "xnb0", "xnb1", "r320", "r321", "xn2tP",
                    "xntP0", "xntP1", "rtP0", "rtP1", "qtP0", "qtP1",
                    "kvKP0", "kvKP1", "kvVP0", "kvVP1",
                    "ebP0", "ebP1", "ptbP0", "ptbP1", "PV0", "PV1",
                    "dgb0", "dgb1", "dgb2", "dgb3", "otP0", "otP1",
                    "routP0", "routP1", "PS0", "PS1", "PP0", "PP1",
                    "PO0", "PO1", "PTb0", "wqkv_b",
                    "stats0", "stats1", "stats2", "stats3", "stats4",
                    "negmx0", "negmx1", "negmx2", "negmx3",
                    "ssum0", "ssum1", "ssum2", "ssum3",
                    "srec0", "srec1", "srec2", "srec3",
                    "ev1zb", "ev2zb", "oddtb",
                )}

                R["PV1"] = R["PV0"]

                def res(base, i=None):
                    return R[base + ("" if i is None else str(i))]

                P.dma("sy", "swa", gm_sb[:], gmd.ap())
                P.dma("sy", "swa", b1_sb[:], b1md.ap())
                P.dma("sy", "swa", b2_sb[:], b2md.ap())
                P.dma("sy", "swa", idf[:], identd.ap())
                P.dma("sy", "swa", xbuf0[0:8, :], xled.ap())
                P.dma("sy", "swa", xbuf1[0:8, :], xlod.ap())
                P.dma("sy", "swa", r320[:], xhd.ap())
                commit_dma(P, "swa", "pload",
                           writes=[R["xbuf0"], R["xbuf1"], R["r320"]])

                P.dma("gp", "swg", idb[:], identd.ap())
                P.dma("gp", "swg", m0b[:], m0d.ap())
                P.dma("gp", "swg", mrb[:], mrd.ap())
                P.dma("gp", "swg", wo_b[:],
                      wod.ap().rearrange("(a b) c -> b a c", b=128))
                P.dma("gp", "swg", sc8[:], bass.AP(scd, 0, [[0, 8], [1, 1]]))
                commit_dma(P, "swg", "pcast", writes=[])

                for dcc in range(DC):
                    for hf_ in range(2):
                        sg = (dcc * 2 + hf_) % 2
                        stg = rtP[sg]
                        stgr = res("rtP", sg)
                        acquire(P, "sy", writes=[stgr])
                        P.dma("sy", f"sbi{sg}", stg[:, 0:1536],
                              wqkvd.ap()[dcc * 128:(dcc + 1) * 128,
                                         hf_ * 1536:(hf_ + 1) * 1536])
                        commit_dma(P, f"sbi{sg}", f"wql{dcc}_{hf_}",
                                   writes=[stgr])
                        acquire(P, "ve", reads=[stgr], writes=[R["wqkv_b"]])
                        P.wait_ck("ve", "pload")
                        if hf_ == 0:
                            P.op("ve", "tensor_scalar",
                                 wqkv_b[:, dcc * 3072: dcc * 3072 + 1024],
                                 stg[:, 0:1024], gm_sb[:, dcc:dcc + 1], 0.125,
                                 ALU.mult, ALU.mult)
                            P.op("ve", "tensor_scalar",
                                 wqkv_b[:, dcc * 3072 + 1024:
                                        dcc * 3072 + 1536],
                                 stg[:, 1024:1536], gm_sb[:, dcc:dcc + 1],
                                 None, ALU.mult, inc=True)
                        else:
                            P.op("ve", "tensor_scalar",
                                 wqkv_b[:, dcc * 3072 + 1536:
                                        dcc * 3072 + 3072],
                                 stg[:, 0:1536], gm_sb[:, dcc:dcc + 1],
                                 None, ALU.mult, inc=True)
                        commit(P, "ve", f"wqc{dcc}_{hf_}", reads=[stgr],
                               writes=[R["wqkv_b"]])

                for li, (src, dst, xres, dres) in enumerate(
                        ((xbuf0, xc0, R["xbuf0"], R["xc0"]),
                         (xbuf1, xc1, R["xbuf1"], R["xc1"]))):
                    sl = 3
                    cb = 32 * li
                    def lc(c):
                        return stats[0:8, 48 + li * 16 + c: 48 + li * 16 + c + 1]
                    sres = R["stats" + str(3 + li)]
                    sqi = (xnb0, xnb1)[li]
                    sqs = (xn2b0, xn2b1)[li]
                    sqir = res("xnb", li)
                    sqsr = res("xn2b", li)
                    acquire(P, "se", reads=[xres], writes=[sqir, sqsr, sres])
                    P.op("se", "activation", sqi[0:8, :], src[0:8, :],
                         AF.Identity, accum_out=lc(0))
                    P.op("se", "activation", sqs[0:8, :], src[0:8, :],
                         AF.Square, accum_out=lc(3), inc=True)
                    commit(P, "se", f"lsq{li}", reads=[xres],
                           writes=[sqir, sqsr])

                    acquire(P, "ve", writes=[sres])
                    P.wait_ck("ve", f"lsq{li}")
                    P.op("ve", "tensor_scalar", lc(1), lc(0), 1.0 / D, None, ALU.mult)
                    P.op("ve", "tensor_scalar", lc(4), lc(3), 1.0 / D, None, ALU.mult)
                    P.op("ve", "tensor_scalar", lc(15), lc(0), lc(0),
                         1.0 / (D * D), ALU.mult, ALU.mult)
                    P.op("ve", "tensor_scalar", lc(2), lc(0), -1.0 / D, None,
                         ALU.mult)
                    P.op("ve", "tensor_tensor", lc(5), lc(4), lc(15), ALU.subtract)
                    P.op("ve", "tensor_scalar", lc(14), lc(5), 1e-5, None, ALU.add,
                         inc=True)
                    commit(P, "ve", f"lst{li}", writes=[sres])

                    P.wait_ck("se", f"lst{li}")
                    P.op("se", "activation", lc(12), lc(14), AF.Ln)
                    P.op("se", "activation", lc(6), lc(12), AF.Exp, scale=-0.5,
                         inc=True)
                    commit(P, "se", f"lrs{li}", writes=[sres])
                    P.wait_ck("ve", f"lrs{li}")
                    P.op("ve", "tensor_tensor", lc(15), lc(6), lc(6), ALU.mult)
                    P.op("ve", "tensor_tensor", lc(13), lc(15), lc(14), ALU.mult)
                    P.op("ve", "tensor_scalar", lc(12), lc(13), -0.5, 1.5,
                         ALU.mult, ALU.add)
                    P.op("ve", "tensor_tensor", lc(7), lc(6), lc(12), ALU.mult)
                    P.op("ve", "tensor_tensor", lc(8), lc(2), lc(7), ALU.mult)
                    commit(P, "ve", f"lnm{li}", writes=[sres])
                    acquire(P, "ve", reads=[xres], writes=[dres])
                    P.op("ve", "tensor_scalar", dst[0:8, :], src[0:8, :],
                         lc(7), lc(8), ALU.mult, ALU.add, inc=True)
                    commit(P, "ve", f"lxn{li}", reads=[xres, sres],
                           writes=[dres])

                acquire(P, "ve", writes=[R["ev1zb"]])
                P.op("ve", "memset", ev1zb[:], 0.0, inc=True)
                commit(P, "ve", "evz1", writes=[R["ev1zb"]])
                acquire(P, "gp", reads=[R["xc0"]], writes=[R["ev1zb"]])
                P.wait_ck("gp", "lxn0")
                for i in range(4):
                    P.dma("gp", "slt", ev1zb[i:i + 1, i * 1024:(i + 1) * 1024],
                          xc0.ap()[i:i + 1, 0:1024])

                acquire(P, "ve", writes=[R["ev2zb"]])
                P.op("ve", "memset", ev2zb[:], 0.0, inc=True)
                commit(P, "ve", "evz2", writes=[R["ev2zb"]])
                acquire(P, "gp", reads=[R["xc0"]], writes=[R["ev2zb"]])
                for i in range(4):
                    for r in range(4):
                        P.dma("gp", "slt",
                              ev2zb[i:i + 1,
                                    r * 1024 + i * 256: r * 1024 + i * 256 + 256],
                              xc0.ap()[4 + i:5 + i, r:1024:4],
                              allow_slow_non_contiguous=True)
                commit_dma(P, "slt", "evd2",
                           reads=[R["xc0"]],
                           writes=[R["ev1zb"], R["ev2zb"]])

                acquire(P, "te", reads=[R["xc1"]], writes=[R["PS0"]])
                P.wait_ck("te", "lxn1")
                P.wait_ck("te", "pload")
                for dcc in range(DC):
                    P.op("te", "transpose",
                         PS0[:, dcc * 8: dcc * 8 + 8],
                         xc1[0:8, dcc * 128:(dcc + 1) * 128],
                         idf[0:8, 0:8], inc=(dcc == DC - 1))
                commit(P, "te", "oddt", reads=[R["xc1"]], writes=[R["PS0"]])
                acquire(P, "ve", reads=[R["PS0"]], writes=[R["oddtb"]])
                P.wait_ck("ve", "oddt")
                P.op("ve", "tensor_copy", oddtb[:], PS0[:, 0:64], inc=True)
                commit(P, "ve", "oddte", reads=[R["PS0"]], writes=[R["oddtb"]])

                def emit_prep(b):
                    for _ in emit_prep_gen(b):
                        pass

                def emit_prep_gen(b):
                    halo = b < 0
                    p = b % 2 if not halo else 1
                    slot = 2 if halo else p
                    xs = r320 if halo else xbuf[p]
                    xsr = R["r320"] if halo else res("xbuf", p)
                    xcs = xc1 if halo else xcb[p]
                    xcr = R["xc1"] if halo else res("xc", p)
                    x2s = xn2b1 if halo else xn2b[p]
                    x2r = R["xn2b1"] if halo else res("xn2b", p)
                    tag = "h" if halo else str(b)

                    def sc_(c):
                        return scol(slot, c)

                    if not halo:
                        acquire(P, "sy", writes=[xsr])
                        P.dma("sy", f"sx{p}", xs[:],
                              xd.ap()[b * 128:(b + 1) * 128, :])
                        commit_dma(P, f"sx{p}", f"ldx{b}", writes=[xsr])
                    yield None

                    sqi = xnb[1] if halo else xnb[p]
                    sqir = res("xnb", 1 if halo else p)
                    acquire(P, "se", reads=[xsr],
                            writes=[res("stats", slot), sqir, x2r])
                    P.op("se", "activation", sqi[:], xs[:],
                         AF.Identity, accum_out=sc_(0))
                    P.op("se", "activation", x2s[:], xs[:], AF.Square,
                         accum_out=sc_(3), inc=True)
                    commit(P, "se", f"sq{tag}", reads=[xsr],
                           writes=[sqir, x2r])
                    yield None

                    acquire(P, "ve", writes=[res("stats", slot)])
                    P.wait_ck("ve", f"sq{tag}")
                    P.op("ve", "tensor_scalar", sc_(1), sc_(0), 1.0 / D, None, ALU.mult)
                    P.op("ve", "tensor_scalar", sc_(4), sc_(3), 1.0 / D, None, ALU.mult)
                    P.op("ve", "tensor_scalar", sc_(15), sc_(0), sc_(0),
                         1.0 / (D * D), ALU.mult, ALU.mult)
                    P.op("ve", "tensor_scalar", sc_(2), sc_(0), -1.0 / D, None,
                         ALU.mult)
                    P.op("ve", "tensor_tensor", sc_(5), sc_(4), sc_(15),
                         ALU.subtract)
                    P.op("ve", "tensor_scalar", sc_(14), sc_(5), 1e-5, None,
                         ALU.add, inc=True)
                    commit(P, "ve", f"st{tag}",
                           writes=[res("stats", slot)])
                    yield None

                    P.wait_ck("se", f"st{tag}")
                    P.op("se", "activation", sc_(12), sc_(14), AF.Ln)
                    P.op("se", "activation", sc_(6), sc_(12), AF.Exp, scale=-0.5,
                         inc=True)
                    commit(P, "se", f"ch_{tag}", writes=[res("stats", slot)])
                    yield None

                    acquire(P, "ve", reads=[xsr], writes=[xcr])
                    P.wait_ck("ve", f"ch_{tag}")
                    P.op("ve", "tensor_tensor", sc_(15), sc_(6), sc_(6), ALU.mult)
                    P.op("ve", "tensor_tensor", sc_(13), sc_(15), sc_(14), ALU.mult)
                    P.op("ve", "tensor_scalar", sc_(12), sc_(13), -0.5, 1.5,
                         ALU.mult, ALU.add)
                    P.op("ve", "tensor_tensor", sc_(7), sc_(6), sc_(12), ALU.mult)
                    P.op("ve", "tensor_tensor", sc_(3), sc_(7), sc_(7), ALU.mult)
                    P.op("ve", "tensor_scalar", sc_(11), sc_(7), 1.0, None,
                         ALU.add)
                    P.op("ve", "tensor_scalar", sc_(8), sc_(5), sc_(3), 1e-6,
                         ALU.mult, ALU.add)
                    P.op("ve", "tensor_scalar", xcs[:], xs[:], sc_(2), None,
                         ALU.add, inc=True)
                    commit(P, "ve", f"ch2_{tag}", reads=[xsr],
                           writes=[res("stats", slot), xcr])
                    yield None

                    P.wait_ck("se", f"ch2_{tag}")
                    P.op("se", "activation", sc_(12), sc_(8), AF.Ln)
                    P.op("se", "activation", sc_(9), sc_(12), AF.Exp, scale=-0.5,
                         inc=True)
                    commit(P, "se", f"ch3_{tag}", writes=[res("stats", slot)])
                    yield None

                    acquire(P, "ve", reads=[xcr], writes=[x2r])
                    P.wait_ck("ve", f"ch3_{tag}")
                    P.op("ve", "tensor_tensor", sc_(0), sc_(9), sc_(9), ALU.mult)
                    P.op("ve", "tensor_tensor", sc_(13), sc_(0), sc_(8), ALU.mult)
                    P.op("ve", "tensor_scalar", sc_(4), sc_(13), -0.5, 1.5,
                         ALU.mult, ALU.add)
                    P.op("ve", "tensor_tensor", sc_(2), sc_(9), sc_(4), ALU.mult)
                    P.op("ve", "tensor_tensor", sc_(10), sc_(7), sc_(2), ALU.mult)
                    P.op("ve", "tensor_scalar", x2s[:], xcs[:], sc_(10), None,
                         ALU.mult, inc=True)
                    commit(P, "ve", f"x2{tag}",
                           reads=[xcr, res("stats", slot)], writes=[x2r])
                    yield None

                    if not halo:
                        acquire(P, "ve", reads=[xcr],
                                writes=[res("xnb", p), res("r32", p)])
                        P.op("ve", "tensor_scalar", xnb[p][:], xcs[:], sc_(7),
                             None, ALU.mult)
                        P.op("ve", "tensor_scalar", r32[p][:], xcs[:], sc_(11),
                             sc_(1), ALU.mult, ALU.add, inc=True)
                        commit(P, "ve", f"xnr{tag}",
                               reads=[xcr, res("stats", slot)],
                               writes=[res("xnb", p), res("r32", p)])


                NPAIR = NBLK // 2

                def emit_tp(tb, bank, bankres, src_t, srcres, dst_t, dstres,
                            off, ident, tag):
                    acquire(P, "te", reads=[srcres], writes=[bankres])
                    P.wait_ck("te", "pcast")
                    for dcc in range(DC):
                        P.op("te", "transpose",
                             bank[:, dcc * 128:(dcc + 1) * 128],
                             src_t[:, dcc * 128:(dcc + 1) * 128], ident,
                             inc=(dcc == DC - 1))
                    commit(P, "te", f"tp{tag}", reads=[srcres],
                           writes=[bankres])
                    acquire(P, "ve", reads=[bankres], writes=[dstres])
                    P.wait_ck("ve", f"tp{tag}")
                    P.op("ve", "tensor_copy",
                         bass.AP(dst_t, off, [[2048, 128], [256, 8], [1, 128]]),
                         bass.AP(bank, 0, [[1024, 128], [128, 8], [1, 128]]),
                         inc=True)
                    commit(P, "ve", f"tpe{tag}", reads=[bankres],
                           writes=[dstres])

                def emit_rt(b, half, tag):
                    p = b % 2
                    qj = (b // 2) % 2
                    bres = res("PO", half)
                    acquire(P, "te", reads=[res("r32", p)], writes=[bres])
                    for di in range(4):
                        dcc = half * 4 + di
                        P.op("te", "transpose",
                             PO[half][:, di * 128:(di + 1) * 128],
                             r32[p][:, dcc * 128:(dcc + 1) * 128], idf[:],
                             inc=(di == 3))
                    commit(P, "te", f"trt{tag}", reads=[res("r32", p)],
                           writes=[bres])
                    acquire(P, "ve", reads=[bres], writes=[res("rtP", qj)])
                    P.wait_ck("ve", f"trt{tag}")
                    P.op("ve", "tensor_copy",
                         bass.AP(rtP[qj], half * 4 * 256 + p * 128,
                                 [[2048, 128], [256, 4], [1, 128]]),
                         bass.AP(PO[half], 0, [[512, 128], [128, 4], [1, 128]]),
                         inc=True)
                    commit(P, "ve", f"ert{tag}", reads=[bres],
                           writes=[res("rtP", qj)])

                def emit_pre_pair(j):
                    q = j % 2
                    B0, B1 = 2 * j, 2 * j + 1
                    emit_tp(j, PTb0, R["PTb0"], xn2b0, R["xn2b0"],
                            xn2tP, R["xn2tP"], 0, idb[:], f"x2_{B0}")
                    yield None
                    emit_tp(j, PTb0, R["PTb0"], xn2b1, R["xn2b1"],
                            xn2tP, R["xn2tP"], 128, idb[:], f"x2_{B1}")
                    yield None
                    for L in range(8):
                        m0 = 2 * L
                        bank = L % 2
                        bres = res("PO", bank)
                        acquire(P, "te", reads=[R["xn2tP"], R["wqkv_b"]],
                                writes=[bres])
                        for mi in range(2):
                            m = m0 + mi
                            for dcc in range(DC):
                                last = (mi == 1 and dcc == DC - 1)
                                P.op("te", "matmul",
                                     PO[bank][:, mi * 256:(mi + 1) * 256],
                                     wqkv_b[:, dcc * 3072 + m * 128:
                                            dcc * 3072 + m * 128 + 128],
                                     xn2tP[:, dcc * 256:(dcc + 1) * 256],
                                     start=(mi == 0 and dcc == 0),
                                     stop=last, inc=last)
                        commit(P, "te", f"qk{j}_{L}", reads=[R["xn2tP"]],
                               writes=[bres])
                        if L < 4:
                            tres = res("qtP", q)
                            acquire(P, "ve", reads=[bres], writes=[tres])
                            P.wait_ck("ve", f"qk{j}_{L}")
                            P.op("ve", "tensor_copy",
                                 qtP[q][:, L * 512:(L + 1) * 512],
                                 PO[bank][:, 0:512], inc=True)
                            commit(P, "ve", f"eqk{j}_{L}", reads=[bres],
                                   writes=[tres])
                        else:
                            tres = res("kvKP", q)
                            acquire(P, "ve", reads=[bres], writes=[tres])
                            P.wait_ck("ve", f"qk{j}_{L}")
                            P.op("ve", "tensor_copy",
                                 bass.AP(kvKP[q], (L - 4) * 2 * 384 + 128,
                                         [[3072, 128], [384, 2], [1, 256]]),
                                 bass.AP(PO[bank], 0,
                                         [[512, 128], [256, 2], [1, 256]]),
                                 inc=True)
                            commit(P, "ve", f"eqk{j}_{L}", reads=[bres],
                                   writes=[tres])
                        yield None
                    for blk in range(2):
                        for rnd in range(2):
                            bank = rnd
                            bres = res("PO", bank)
                            acquire(P, "te", reads=[R["xn2tP"], R["wqkv_b"]],
                                    writes=[bres])
                            for dcc in range(DC):
                                P.op("te", "matmul", PO[bank][:, 0:512],
                                     xn2tP[:, dcc * 256 + blk * 128:
                                           dcc * 256 + blk * 128 + 128],
                                     wqkv_b[:, dcc * 3072 + 2048 + rnd * 512:
                                            dcc * 3072 + 2048 + rnd * 512 + 512],
                                     start=(dcc == 0), stop=(dcc == DC - 1),
                                     inc=(dcc == DC - 1))
                            commit(P, "te", f"vm{j}_{blk}_{rnd}",
                                   reads=[R["xn2tP"]], writes=[bres])
                            tres = res("kvVP", q)
                            acquire(P, "ve", reads=[bres], writes=[tres])
                            P.wait_ck("ve", f"vm{j}_{blk}_{rnd}")
                            P.op("ve", "tensor_copy",
                                 kvVP[q][:, blk * 1024 + rnd * 512:
                                         blk * 1024 + rnd * 512 + 512],
                                 PO[bank][:, 0:512], inc=True)
                            commit(P, "ve", f"evv{j}_{blk}_{rnd}",
                                   reads=[bres], writes=[tres])
                            yield None
                    emit_tp(j, PTb0, R["PTb0"], xnb0, R["xnb0"],
                            xntP[q], res("xntP", q), 0, idb[:], f"xn_{B0}")
                    yield None
                    emit_tp(j, PTb0, R["PTb0"], xnb1, R["xnb1"],
                            xntP[q], res("xntP", q), 128, idb[:], f"xn_{B1}")
                    yield None
                    for blk in range(2):
                        for half in range(2):
                            emit_rt(2 * j + blk, half, f"{2 * j + blk}_{half}")
                            yield None

                def emit_halo_pre():
                    acquire(P, "te", reads=[R["xn2b1"]], writes=[R["PTb0"]])
                    P.wait_ck("te", "pcast")
                    for dcc in range(DC):
                        P.op("te", "transpose",
                             PTb0[:, dcc * 128:(dcc + 1) * 128],
                             xn2b1[:, dcc * 128:(dcc + 1) * 128], idb[:],
                             inc=(dcc == DC - 1))
                    commit(P, "te", "tph", reads=[R["xn2b1"]],
                           writes=[R["PTb0"]])
                    acquire(P, "ve", reads=[R["PTb0"]], writes=[R["xn2tP"]])
                    P.wait_ck("ve", "tph")
                    P.op("ve", "tensor_copy",
                         bass.AP(xn2tP, 128, [[2048, 128], [256, 8], [1, 128]]),
                         bass.AP(PTb0, 0, [[1024, 128], [128, 8], [1, 128]]),
                         inc=True)
                    commit(P, "ve", "tpeh", reads=[R["PTb0"]],
                           writes=[R["xn2tP"]])
                    for L in range(2):
                        bank = L % 2
                        bres = res("PO", bank)
                        acquire(P, "te", reads=[R["xn2tP"], R["wqkv_b"]],
                                writes=[bres])
                        for mi in range(4):
                            m = 8 + L * 4 + mi
                            for dcc in range(DC):
                                last = (mi == 3 and dcc == DC - 1)
                                P.op("te", "matmul",
                                     PO[bank][:, mi * 128:(mi + 1) * 128],
                                     wqkv_b[:, dcc * 3072 + m * 128:
                                            dcc * 3072 + m * 128 + 128],
                                     xn2tP[:, dcc * 256 + 128:
                                           dcc * 256 + 256],
                                     start=(mi == 0 and dcc == 0),
                                     stop=last, inc=last)
                        commit(P, "te", f"qkh_{L}", reads=[R["xn2tP"]],
                               writes=[bres])
                        acquire(P, "ve", reads=[bres], writes=[res("kvKP", 0)])
                        P.wait_ck("ve", f"qkh_{L}")
                        P.op("ve", "tensor_copy",
                             bass.AP(kvKP[0], L * 4 * 384,
                                     [[3072, 128], [384, 4], [1, 128]]),
                             bass.AP(PO[bank], 0,
                                     [[512, 128], [128, 4], [1, 128]]),
                             inc=True)
                        commit(P, "ve", f"eqkh_{L}", reads=[bres],
                               writes=[res("kvKP", 0)])
                    for rnd in range(2):
                        bank = rnd
                        bres = res("PO", bank)
                        acquire(P, "te", reads=[R["xn2tP"], R["wqkv_b"]],
                                writes=[bres])
                        for dcc in range(DC):
                            P.op("te", "matmul", PO[bank][:, 0:512],
                                 xn2tP[:, dcc * 256 + 128: dcc * 256 + 256],
                                 wqkv_b[:, dcc * 3072 + 2048 + rnd * 512:
                                        dcc * 3072 + 2048 + rnd * 512 + 512],
                                 start=(dcc == 0), stop=(dcc == DC - 1),
                                 inc=(dcc == DC - 1))
                        commit(P, "te", f"vmh_{rnd}", reads=[R["xn2tP"]],
                               writes=[bres])
                        acquire(P, "ve", reads=[bres], writes=[res("kvVP", 1)])
                        P.wait_ck("ve", f"vmh_{rnd}")
                        P.op("ve", "tensor_copy",
                             kvVP[1][:, 1024 + rnd * 512:
                                     1024 + rnd * 512 + 512],
                             PO[bank][:, 0:512], inc=True)
                        commit(P, "ve", f"evvh_{rnd}", reads=[bres],
                               writes=[res("kvVP", 1)])

                def emit_heads(b, feeder, cap, fstart=0):
                    q = (b // 2) % 2
                    p = b % 2
                    mask = m0b if b == 0 else mrb
                    NPH = NH // 2

                    def st_scores(g):
                        gb = g % 2
                        pr = (2 * g) % 4
                        sb = res("PS", gb)
                        acquire(P, "te", reads=[res("qtP", q), res("kvKP", q)],
                                writes=[sb])
                        if g == 0:
                            P.wait_ck("te", "pcast")
                        for i in range(2):
                            h = 2 * g + i
                            po = i * 64
                            P.op("te", "matmul",
                                 PS[gb][:, i * 256:(i + 1) * 256],
                                 idb[:], mask[:, i * 256:(i + 1) * 256],
                                 start=True, stop=False)
                            P.op("te", "matmul",
                                 PS[gb][:, i * 256:(i + 1) * 256],
                                 qtP[q][po:po + 64,
                                        g * 256 + p * 128: g * 256 + p * 128 + 128],
                                 kvKP[q][po:po + 64,
                                         g * 384 + p * 128: g * 384 + p * 128 + 256],
                                 start=False, stop=True, inc=(i == 1))
                        commit(P, "te", f"sc{b}_{g}",
                               reads=[res("qtP", q), res("kvKP", q)],
                               writes=[sb])

                        acquire(P, "ve", reads=[sb], writes=[res("negmx", pr)])
                        P.wait_ck("ve", f"sc{b}_{g}")
                        P.op("ve", "reduce_max", negmx[:, pr:pr + 2],
                             bass.AP(PS[gb], 0, [[512, 128], [256, 2], [1, 256]]),
                             AX, negate=True, inc=True)
                        commit(P, "ve", f"mx{b}_{g}", reads=[sb],
                               writes=[res("negmx", pr)])

                        acquire(P, "se", reads=[sb, res("negmx", pr)],
                                writes=[res("ebP", gb), res("ssum", pr)])
                        for i in range(2):
                            P.op("se", "activation",
                                 ebP[gb][:, i * 256:(i + 1) * 256],
                                 PS[gb][:, i * 256:(i + 1) * 256],
                                 AF.Exp, bias=negmx[:, pr + i:pr + i + 1],
                                 accum_out=ssum[:, pr + i:pr + i + 1],
                                 inc=(i == 1))
                        commit(P, "se", f"ex{b}_{g}", reads=[sb],
                               writes=[res("ebP", gb), res("ssum", pr)])

                    def st_dg(g):
                        pr = (2 * g) % 4
                        acquire(P, "ve", reads=[res("ssum", pr)],
                                writes=[res("srec", pr), res("dgb", pr),
                                        res("dgb", pr + 1)])
                        P.wait_ck("ve", f"ex{b}_{g}")
                        P.op("ve", "reciprocal", srec[:, pr:pr + 2],
                             ssum[:, pr:pr + 2])
                        P.op("ve", "tensor_scalar", dgb[pr][:], idb[:],
                             srec[:, pr:pr + 1], None, ALU.mult)
                        P.op("ve", "tensor_scalar", dgb[pr + 1][:], idb[:],
                             srec[:, pr + 1:pr + 2], None, ALU.mult, inc=True)
                        commit(P, "ve", f"dg{b}_{g}",
                               reads=[res("ssum", pr)],
                               writes=[res("srec", pr), res("dgb", pr),
                                       res("dgb", pr + 1)])

                    def st_pt(g):
                        gb = g % 2
                        pr = (2 * g) % 4
                        pb_ = res("PP", gb)
                        acquire(P, "te", reads=[res("ebP", gb), res("dgb", pr),
                                                res("dgb", pr + 1)],
                                writes=[pb_])
                        for i in range(2):
                            P.op("te", "matmul",
                                 PP[gb][:, i * 256: i * 256 + 128],
                                 ebP[gb][:, i * 256: i * 256 + 128],
                                 dgb[pr + i][:], start=(i == 0), stop=False)
                            P.op("te", "matmul",
                                 PP[gb][:, i * 256 + 128: i * 256 + 256],
                                 ebP[gb][:, i * 256 + 128: i * 256 + 256],
                                 dgb[pr + i][:], start=False, stop=(i == 1),
                                 inc=(i == 1))
                        commit(P, "te", f"pt{b}_{g}",
                               reads=[res("ebP", gb), res("dgb", pr),
                                      res("dgb", pr + 1)],
                               writes=[pb_])

                        acquire(P, "se", reads=[pb_], writes=[res("ptbP", gb)])
                        P.wait_ck("se", f"pt{b}_{g}")
                        P.op("se", "activation", ptbP[gb][:], PP[gb][:, 0:512],
                             AF.Copy, inc=True)
                        commit(P, "se", f"pte{b}_{g}", reads=[pb_],
                               writes=[res("ptbP", gb)])

                    def st_pv(g):
                        gb = g % 2
                        pb_ = res("PV", gb)
                        if p == 0:
                            vprev_t = kvVP[1 - q]
                            vprev_res = res("kvVP", 1 - q)
                            vprev_off = 1024
                        else:
                            vprev_t = kvVP[q]
                            vprev_res = res("kvVP", q)
                            vprev_off = 0
                        acquire(P, "te", reads=[res("kvVP", q), vprev_res,
                                                res("ptbP", gb)],
                                writes=[pb_])
                        for i in range(2):
                            h = 2 * g + i
                            po = i * 64
                            tp = (0, po) if po else None
                            P.op("te", "matmul", PV[gb][po:po + 64, 0:128],
                                 vprev_t[:, vprev_off + h * 64:
                                         vprev_off + h * 64 + 64],
                                 ptbP[gb][:, i * 256: i * 256 + 128],
                                 start=True, stop=False, tile_position=tp)
                            P.op("te", "matmul", PV[gb][po:po + 64, 0:128],
                                 kvVP[q][:, p * 1024 + h * 64:
                                         p * 1024 + h * 64 + 64],
                                 ptbP[gb][:, i * 256 + 128: i * 256 + 256],
                                 start=False, stop=True, tile_position=tp,
                                 inc=(i == 1))
                        commit(P, "te", f"pv{b}_{g}",
                               reads=[res("kvVP", q), vprev_res,
                                      res("ptbP", gb)],
                               writes=[pb_])

                        acquire(P, "se", reads=[pb_], writes=[res("otP", q)])
                        P.wait_ck("se", f"pv{b}_{g}")
                        P.op("se", "activation",
                             otP[q][:, g * 256 + p * 128: g * 256 + p * 128 + 128],
                             PV[gb][:, 0:128], AF.Copy, inc=True)
                        commit(P, "se", f"oe{b}_{g}", reads=[pb_],
                               writes=[res("otP", q)])

                    for it in range(NPH + 3):
                        if 2 <= it <= NPH + 1:
                            st_pt(it - 2)
                        if it >= 3:
                            st_pv(it - 3)
                        if it < NPH:
                            st_scores(it)
                        if 1 <= it <= NPH:
                            st_dg(it - 1)
                        if it >= fstart:
                            feeder.feed(cap)


                def emit_wo_stores(j):
                    q = j % 2
                    for L in range(4):
                        wb_ = res("PP", L % 2)
                        acquire(P, "te", reads=[res("otP", q)], writes=[wb_])
                        if L == 0:
                            P.wait_ck("te", "pcast")
                        for di in range(2):
                            dcc = 2 * L + di
                            for oc in range(DC):
                                last = (di == 1 and oc == DC - 1)
                                P.op("te", "matmul",
                                     PP[L % 2][:, di * 256:(di + 1) * 256],
                                     wo_b[:, oc * 1024 + dcc * 128:
                                          oc * 1024 + dcc * 128 + 128],
                                     otP[q][:, oc * 256:(oc + 1) * 256],
                                     start=(di == 0 and oc == 0),
                                     stop=last, inc=last)
                        commit(P, "te", f"wo{j}_{L}", reads=[res("otP", q)],
                               writes=[wb_])

                        acquire(P, "ve", reads=[wb_, res("rtP", q)],
                                writes=[res("routP", q)])
                        P.wait_ck("ve", f"wo{j}_{L}")
                        P.op("ve", "scalar_tensor_tensor",
                             routP[q][:, L * 512:(L + 1) * 512],
                             PP[L % 2][:, 0:512], 1.0,
                             rtP[q][:, L * 512:(L + 1) * 512],
                             ALU.mult, ALU.add, inc=True)
                        commit(P, "ve", f"ewo{j}_{L}",
                               reads=[wb_, res("rtP", q)], writes=[res("routP", q)])

                    acquire(P, "gp", reads=[res("routP", q), res("xntP", q)])
                    P.dma("gp", f"sst{j % 2}",
                          scr_r.ap().rearrange("(a b) c -> b a c", b=128)
                          [:, :, j * 256:(j + 1) * 256],
                          bass.AP(routP[q], 0, [[2048, 128], [256, 8], [1, 256]]))
                    P.dma("gp", f"sst{j % 2}",
                          scr_xn.ap().rearrange("(a b) c -> b a c", b=128)
                          [:, :, j * 256:(j + 1) * 256],
                          bass.AP(xntP[q], 0, [[2048, 128], [256, 8], [1, 256]]))
                    commit_dma(P, f"sst{j % 2}", f"str{j}",
                               reads=[res("routP", q), res("xntP", q)])

                class Feeder:
                    def __init__(self, gen):
                        self.gen = gen
                        self.count = 0
                        self.done = gen is None

                    def feed(self, cap):
                        if not self.done and self.count < cap:
                            try:
                                next(self.gen)
                                self.count += 1
                            except StopIteration:
                                self.done = True

                    def drain(self):
                        if not self.done:
                            for _ in self.gen:
                                pass
                            self.done = True

                emit_prep(-1)
                emit_prep(0)
                emit_halo_pre()
                emit_prep(1)
                Feeder(emit_pre_pair(0)).drain()
                for j in range(NPAIR):
                    gens = [emit_prep_gen(b2) for b2 in (2 * j + 2, 2 * j + 3)
                            if b2 < NBLK]
                    alive = True
                    while alive:
                        alive = False
                        for gg in gens:
                            try:
                                next(gg)
                                alive = True
                            except StopIteration:
                                pass
                    if j > 0:
                        emit_wo_stores(j - 1)
                    if j < NPAIR - 1:
                        q = j % 2
                        acquire(P, "ve", reads=[res("kvKP", q)],
                                writes=[res("kvKP", 1 - q)])
                        P.op("ve", "tensor_copy",
                             bass.AP(kvKP[1 - q], 0,
                                     [[3072, 128], [384, 8], [1, 128]]),
                             bass.AP(kvKP[q], 256,
                                     [[3072, 128], [384, 8], [1, 128]]),
                             inc=True)
                        commit(P, "ve", f"kvc{j}", reads=[res("kvKP", q)],
                               writes=[res("kvKP", 1 - q)])
                    feeder = Feeder(emit_pre_pair(j + 1)
                                    if j + 1 < NPAIR else None)
                    emit_heads(2 * j, feeder, 10)
                    emit_heads(2 * j + 1, feeder, 14)
                    feeder.drain()
                emit_wo_stores(NPAIR - 1)


                for e in ("te", "ve", "se"):
                    P.ckpt(e, f"Aend_{e}")
                if dbgst is not None:
                    for e in ("ve", "se"):
                        P.wait_ck("gp", f"Aend_{e}")
                    P.dma("gp", "slt", dbgst.ap(), stats[:])
                    commit_dma(P, "slt", "dbgstd")
                P.dma_ckpt("sst0", "Ast0")
                P.dma_ckpt("sst1", "Ast1")

            Pc = Prog("count", ck)
            emit_A(Pc)
            totals_A = (dict(Pc.cnt), dict(Pc.dcnt))
            track_A = ({e: {tn: dict(d) for tn, d in sh.items()}
                        for e, sh in Pc.shadow.items()}, dict(Pc.self_hw))

            with nc.Block() as blk:
                @blk.tensor
                def _(e):
                    Px = Prog("emit", ck, "te", {"te": e}, esem, dsem)
                    emit_A(Px)
                    assert Px.cnt == totals_A[0]

                @blk.vector
                def _(e):
                    Px = Prog("emit", ck, "ve", {"ve": e}, esem, dsem)
                    emit_A(Px)

                @blk.scalar
                def _(e):
                    Px = Prog("emit", ck, "se", {"se": e}, esem, dsem)
                    emit_A(Px)

                @blk.sync
                def _(e):
                    Px = Prog("emit", ck, "sy", {"sy": e}, esem, dsem)
                    emit_A(Px)

                @blk.gpsimd
                def _(e):
                    Px = Prog("emit", ck, "gp", {"gp": e}, esem, dsem)
                    emit_A(Px)
                    assert Px.dcnt == totals_A[1]

        with contextlib.ExitStack() as stB:
            def sbB(name, shape, dtype):
                return stB.enter_context(nc.sbuf_tensor(name, shape, dtype))

            def pbB(name, shape, dtype):
                return stB.enter_context(nc.psum_tensor(name, shape, dtype))

            w1t_b = sbB("w1t_b", [128, DC * HF], BF16)
            w2t_b = sbB("w2t_b", [128, HCN * D], BF16)
            xnt20 = sbB("xnt20", [128, DC * TB], BF16)
            xnt21 = sbB("xnt21", [128, DC * TB], BF16)
            rtb0 = sbB("rtb0", [128, DC * TB], F32)
            rtb1 = sbB("rtb1", [128, DC * TB], F32)
            hbf0 = sbB("hbf0", [128, TB], BF16)
            hbf1 = sbB("hbf1", [128, TB], BF16)
            u_bf = sbB("u_bf", [4, TB], BF16)
            t2_bf = sbB("t2_bf", [4, 4 * TB], BF16)
            outb0 = sbB("outb0", [128, DC * TB], F32)
            outb1 = sbB("outb1", [128, DC * TB], F32)
            H0 = pbB("H0", [128, 512], F32)
            H1 = pbB("H1", [128, 512], F32)
            T2a = pbB("T2a", [128, 512], F32)
            T2b = pbB("T2b", [128, 512], F32)
            OUT0 = pbB("OUT0", [128, 512], F32)
            OUT1 = pbB("OUT1", [128, 512], F32)
            OUT2 = pbB("OUT2", [128, 512], F32)
            OUT3 = pbB("OUT3", [128, 512], F32)

            xnt2 = (xnt20, xnt21)
            rtb = (rtb0, rtb1)
            hbf = (hbf0, hbf1)
            outb = (outb0, outb1)
            HB = (H0, H1)
            T2 = (T2a, T2b)
            OUT = (OUT0, OUT1, OUT2, OUT3)

            def emit_B(P):
                R = {n: Res() for n in (
                    "xnt20", "xnt21", "rtb0", "rtb1", "hbf0", "hbf1",
                    "u_bf", "t2_bf", "outb0", "outb1",
                    "H0", "H1", "T2a", "T2b", "OUT0", "OUT1", "OUT2", "OUT3",
                )}

                def res(base, i=None):
                    return R[base + ("" if i is None else str(i))]

                for e in ("te", "ve", "se", "sy", "gp"):
                    for o in ("te", "ve", "se"):
                        if o != e:
                            P.wait_ck(e, f"Aend_{o}")
                    for n in ("Ast0", "Ast1", "pload", "pcast",
                              "evd2", "oddte"):
                        P.wait_ck(e, n)

                def w1_chunk(c):
                    P.dma("gp", f"swA{c % 8}",
                          w1t_b[:, c * 1024:(c + 1) * 1024],
                          w1td.ap().rearrange("(a b) c -> b a c", b=128)
                          [:, :, c * 128:(c + 1) * 128])

                def w2_chunk(c):
                    P.dma("gp", f"swB{c % 8}",
                          w2t_b[:, c * 1024:(c + 1) * 1024],
                          w2td.ap().rearrange("(a b) c -> b a c", b=128)
                          [:, c:c + 1, :])

                for c in range(8):
                    w1_chunk(c)
                    w2_chunk(c)

                for s in range(NST):
                    q = s % 2
                    t0 = s * TB

                    acquire(P, "sy", writes=[res("xnt2", q), res("rtb", q)])
                    P.dma("sy", f"sbi{q}",
                          bass.AP(xnt2[q], 0, [[2048, 128], [256, 8], [1, 256]]),
                          scr_xn.ap().rearrange("(a b) c -> b a c", b=128)
                          [:, :, t0:t0 + TB])
                    P.dma("sy", f"sbi{q}",
                          bass.AP(rtb[q], 0, [[2048, 128], [256, 8], [1, 256]]),
                          scr_r.ap().rearrange("(a b) c -> b a c", b=128)
                          [:, :, t0:t0 + TB])
                    commit_dma(P, f"sbi{q}", f"bl{s}",
                               writes=[res("xnt2", q), res("rtb", q)])

                    acquire(P, "te", reads=[res("xnt2", q)], writes=[R["T2a"]])
                    P.wait_ck("te", f"bl{s}")
                    for dcc in range(DC):
                        P.op("te", "matmul", T2a[0:4, 0:256],
                             oddtb[:, dcc * 8: dcc * 8 + 4],
                             xnt2[q][:, dcc * 256:(dcc + 1) * 256],
                             start=(dcc == 0), stop=(dcc == DC - 1),
                             inc=(dcc == DC - 1))
                    commit(P, "te", f"umm{s}", reads=[res("xnt2", q)],
                           writes=[R["T2a"]])
                    acquire(P, "ve", reads=[R["T2a"]], writes=[R["u_bf"]])
                    P.wait_ck("ve", f"umm{s}")
                    P.op("ve", "tensor_scalar", u_bf[:], T2a[0:4, 0:256],
                         sc8[0:4, 0:1], None, ALU.mult, inc=True)
                    commit(P, "ve", f"uev{s}", reads=[R["T2a"]],
                           writes=[R["u_bf"]])

                    def emit_w1(hcc):
                            hp = hcc % 2
                            r = hcc // 8
                            qc2 = hcc % 8
                            hb_ = res("H", hp)
                            acquire(P, "te", reads=[res("xnt2", q), R["u_bf"]],
                                    writes=[hb_])
                            if s == 0:
                                P.wait_dma("te", f"swA{hcc % 8}",
                                           16 * (hcc // 8 + 1))
                            for dcc in range(DC):
                                P.op("te", "matmul", HB[hp][:, 0:256],
                                     w1t_b[:, hcc * 1024 + dcc * 128:
                                           hcc * 1024 + dcc * 128 + 128],
                                     xnt2[q][:, dcc * 256:(dcc + 1) * 256],
                                     start=(dcc == 0), stop=False)
                            P.op("te", "matmul", HB[hp][:, 0:256],
                                 ev1zb[0:4, hcc * 128:(hcc + 1) * 128],
                                 u_bf[0:4, :], start=False, stop=True, inc=True)
                            commit(P, "te", f"w1{s}_{hcc}",
                                   reads=[res("xnt2", q), R["u_bf"]], writes=[hb_])
                            if s == 0 and hcc + 8 < HCN:
                                P.wait_ck("gp", f"w1{s}_{hcc}")
                                w1_chunk(hcc + 8)


                    def emit_gt(hcc):
                            hp = hcc % 2
                            r = hcc // 8
                            qc2 = hcc % 8
                            hb_ = res("H", hp)
                            acquire(P, "se", reads=[hb_], writes=[res("hbf", hp)])
                            P.wait_ck("se", f"w1{s}_{hcc}")
                            P.op("se", "activation", hbf[hp][:], HB[hp][:, 0:256],
                                 AF.Gelu, bias=b1_sb[:, hcc:hcc + 1], inc=True)
                            commit(P, "se", f"ge{s}_{hcc}", reads=[hb_],
                                   writes=[res("hbf", hp)])

                            t2bank = T2[r // 2]
                            t2res = res("T2", ("a", "b")[r // 2])
                            acquire(P, "te", reads=[res("hbf", hp)],
                                    writes=[t2res] + [res("OUT", i) for i in range(4)])
                            P.wait_ck("te", f"ge{s}_{hcc}")
                            if s == 0:
                                P.wait_dma("te", f"swB{hcc % 8}",
                                           16 * (hcc // 8 + 1))
                            P.op("te", "matmul",
                                 t2bank[0:4, (r % 2) * 256:(r % 2) * 256 + 256],
                                 oddtb[:, qc2 * 8 + 4: qc2 * 8 + 8],
                                 hbf[hp][:, 0:256],
                                 start=(qc2 == 0 and r % 2 == 0),
                                 stop=(qc2 == 7 and r % 2 == 1))
                            for oc in range(DC):
                                last = (oc == DC - 1)
                                P.op("te", "matmul",
                                     OUT[oc // 2][:, (oc % 2) * 256:(oc % 2) * 256 + 256],
                                     w2t_b[:, hcc * 1024 + oc * 128:
                                           hcc * 1024 + oc * 128 + 128],
                                     hbf[hp][:, 0:256],
                                     start=(hcc == 0 and oc % 2 == 0),
                                     stop=False, inc=last)
                            commit(P, "te", f"t2w2{s}_{hcc}",
                                   reads=[res("hbf", hp)],
                                   writes=[t2res] + [res("OUT", i) for i in range(4)])
                            if s == 0 and hcc + 8 < HCN:
                                P.wait_ck("gp", f"t2w2{s}_{hcc}")
                                w2_chunk(hcc + 8)

                    emit_w1(0)
                    for hcc in range(HCN):
                        if hcc + 1 < HCN:
                            emit_w1(hcc + 1)
                        emit_gt(hcc)

                    acquire(P, "ve", reads=[R["T2a"], R["T2b"]],
                            writes=[R["t2_bf"]])
                    P.wait_ck("ve", f"t2w2{s}_{HCN - 1}")
                    P.op("ve", "tensor_scalar", t2_bf[0:4, 0:512],
                         T2a[0:4, 0:512], sc8[0:4, 0:1], None, ALU.mult)
                    P.op("ve", "tensor_scalar", t2_bf[0:4, 512:1024],
                         T2b[0:4, 0:512], sc8[0:4, 0:1], None, ALU.mult,
                         inc=True)
                    commit(P, "ve", f"t2e{s}", reads=[R["T2a"], R["T2b"]],
                           writes=[R["t2_bf"]])

                    acquire(P, "te", reads=[R["t2_bf"]],
                            writes=[res("OUT", i) for i in range(4)])
                    P.wait_ck("te", f"t2e{s}")
                    for oc in range(DC):
                        for r in range(4):
                            last = (oc == DC - 1 and r == 3)
                            P.op("te", "matmul",
                                 OUT[oc // 2][:, (oc % 2) * 256:(oc % 2) * 256 + 256],
                                 ev2zb[0:4, r * 1024 + oc * 128:
                                       r * 1024 + oc * 128 + 128],
                                 t2_bf[0:4, r * 256:(r + 1) * 256],
                                 start=False,
                                 stop=(r == 3 and oc % 2 == 1), inc=last)
                    commit(P, "te", f"corr{s}", reads=[R["t2_bf"]],
                           writes=[res("OUT", i) for i in range(4)])

                    acquire(P, "ve",
                            reads=[res("rtb", q)] + [res("OUT", i) for i in range(4)],
                            writes=[res("outb", q)])
                    P.wait_ck("ve", f"corr{s}")
                    for oc in range(DC):
                        P.op("ve", "scalar_tensor_tensor",
                             outb[q][:, oc * 256:(oc + 1) * 256],
                             OUT[oc // 2][:, (oc % 2) * 256:(oc % 2) * 256 + 256],
                             b2_sb[:, oc:oc + 1],
                             rtb[q][:, oc * 256:(oc + 1) * 256],
                             ALU.add, ALU.add, inc=(oc == DC - 1))
                    commit(P, "ve", f"oute{s}",
                           reads=[res("rtb", q)] + [res("OUT", i) for i in range(4)],
                           writes=[res("outb", q)])

                    acquire(P, "gp", reads=[res("outb", q)])
                    P.dma("gp", f"sso{q}",
                          outd.ap().rearrange("(a b) c -> b a c", b=128)
                          [:, :, t0:t0 + TB],
                          bass.AP(outb[q], 0, [[2048, 128], [256, 8], [1, 256]]))
                    commit_dma(P, f"sso{q}", f"os{s}", reads=[res("outb", q)])

                P.wait_dma("gp", "sso0", P.dcnt["sso0"])
                P.wait_dma("gp", "sso1", P.dcnt["sso1"])

            Pc2 = Prog("count", ck)
            Pc2.cnt = dict(totals_A[0])
            Pc2.dcnt = dict(totals_A[1])
            Pc2.shadow = {e: {tn: dict(d) for tn, d in sh.items()}
                          for e, sh in track_A[0].items()}
            Pc2.self_hw = dict(track_A[1])
            emit_B(Pc2)
            totals_B = (dict(Pc2.cnt), dict(Pc2.dcnt))

            def mkprog(eng, e):
                Px = Prog("emit", ck, eng, {eng: e}, esem, dsem)
                Px.cnt = dict(totals_A[0])
                Px.dcnt = dict(totals_A[1])
                Px.shadow = {e2: {tn: dict(d) for tn, d in sh.items()}
                             for e2, sh in track_A[0].items()}
                Px.self_hw = dict(track_A[1])
                return Px

            with nc.Block() as blk2:
                @blk2.tensor
                def _(e):
                    Px = mkprog("te", e)
                    emit_B(Px)
                    assert Px.cnt == totals_B[0]

                @blk2.vector
                def _(e):
                    emit_B(mkprog("ve", e))

                @blk2.scalar
                def _(e):
                    emit_B(mkprog("se", e))

                @blk2.sync
                def _(e):
                    emit_B(mkprog("sy", e))

                @blk2.gpsimd
                def _(e):
                    Px = mkprog("gp", e)
                    emit_B(Px)
                    assert Px.dcnt == totals_B[1]

    return nc


def _masks():
    i = np.arange(128)[:, None]
    j = np.arange(256)[None, :]
    base = (j > i) & (j <= i + 128)
    first = base & (j >= 128)
    mb = np.where(base, 0.0, -1e30).astype(np.float32)
    mf = np.where(first, 0.0, -1e30).astype(np.float32)
    return mb, mf


def kernel(L0_attn_x, L0_attn_g, L0_attn_Wqkv, L0_attn_Wo,
           L0_ffn_W1, L0_ffn_b1, L0_ffn_W2, L0_ffn_b2, L0_ffn_scale):
    X = np.ascontiguousarray(np.asarray(L0_attn_x, dtype=np.float32))
    g = np.asarray(L0_attn_g, dtype=np.float32)
    Wqkv = np.ascontiguousarray(np.asarray(L0_attn_Wqkv, dtype=np.float32))
    Wo = np.ascontiguousarray(np.asarray(L0_attn_Wo, dtype=np.float32))
    W1t = np.ascontiguousarray(np.asarray(L0_ffn_W1, dtype=np.float32).T)
    W2t = np.ascontiguousarray(np.asarray(L0_ffn_W2, dtype=np.float32).T)
    b1 = np.asarray(L0_ffn_b1, dtype=np.float32)
    b2 = np.asarray(L0_ffn_b2, dtype=np.float32)
    sc = np.asarray(L0_ffn_scale, dtype=np.float32).reshape(1, 1)

    gm = np.ascontiguousarray(g.reshape(8, 128).T)
    b1m = np.ascontiguousarray(b1.reshape(32, 128).T)
    b2m = np.ascontiguousarray(b2.reshape(8, 128).T)
    ident = np.eye(128, dtype=np.float32)
    mb, mf = _masks()
    mb = np.ascontiguousarray(np.hstack([mb, mb]))
    mf = np.ascontiguousarray(np.hstack([mf, mf]))

    nc = build_nc()
    in_maps = []
    for c in range(8):
        b = c // 4
        t0 = (c % 4) * TC
        xh = (np.zeros((WIN, D), np.float32) if t0 == 0
              else np.ascontiguousarray(X[b, t0 - WIN:t0]))
        in_maps.append({
            "x": np.ascontiguousarray(X[b, t0:t0 + TC]),
            "xh": xh,
            "xle": np.ascontiguousarray(X[b, 1:17:2]),
            "xlo": np.ascontiguousarray(X[b, 2:17:2]),
            "wqkv": Wqkv, "wo": Wo, "w1t": W1t, "w2t": W2t,
            "gm": gm, "b1m": b1m, "b2m": b2m, "scv": sc,
            "mask0": (mf if t0 == 0 else mb), "maskr": mb,
            "ident": ident,
        })

    res = run_bass_kernel_spmd(nc, in_maps, core_ids=list(range(8)))
    out = np.empty((2, 4 * TC, D), np.float32)
    for c in range(8):
        b = c // 4
        t0 = (c % 4) * TC
        out[b, t0:t0 + TC, :] = res.results[c]["out"].T
    return out
